# version 17
# baseline (speedup 1.0000x reference)
import numpy as np
from contextlib import ExitStack

import concourse.bass as bass
import concourse.mybir as mybir
from concourse.bass_utils import run_bass_kernel_spmd

F32 = mybir.dt.float32
F16 = mybir.dt.float16
F8 = mybir.dt.float8e4
I16 = mybir.dt.int16
ALU = mybir.AluOpType
ACT = mybir.ActivationFunctionType

DM = 1024
DI = 2048
L = 2048
DS = 16
NK = DM // 128
NJ = DI // 128
NM = DM // 128
TC = 512
NC_T = L // TC
ENGINES = ("sync", "scalar", "vector", "tensor", "gpsimd")



class Item:
    __slots__ = ("eng", "fn", "deps", "need_inc", "sem_val", "is_dma",
                 "dsem", "dval", "pre_waits", "tag", "extra_inc", "fence")

    def __init__(self, eng, fn, tag=""):
        self.eng = eng
        self.fn = fn
        self.deps = set()
        self.need_inc = False
        self.sem_val = 0
        self.is_dma = False
        self.dsem = None
        self.dval = 0
        self.pre_waits = ()
        self.tag = tag
        self.extra_inc = None
        self.fence = False


class Buf:
    __slots__ = ("writer", "readers", "name")

    def __init__(self, name=""):
        self.writer = None
        self.readers = []
        self.name = name


class _DmaSem:
    __slots__ = ("sem", "uses")

    def __init__(self, sem):
        self.sem = sem
        self.uses = 0


class Sched:
    def __init__(self, nc, es):
        self.nc = nc
        self.items = {e: [] for e in ENGINES}
        self.prog_sem = {e: es.enter_context(nc.semaphore("pg_" + e))
                         for e in ENGINES}
        self.dma_pools = {}
        for e, n in (("sync", 12), ("gpsimd", 12), ("scalar", 4)):
            self.dma_pools[e] = [_DmaSem(es.enter_context(nc.semaphore(f"dm_{e}{i}")))
                                 for i in range(n)]
        self.dma_rr = {e: 0 for e in self.dma_pools}
        self.last_barrier = []
        self._es = es
        self._barn = 0
        self.vfence = False

    def _dep(self, item, dep):
        if dep is None or dep is item:
            return
        if dep.eng == item.eng and not dep.is_dma:
            if SIM_EDGES or dep.fence or dep.eng in EDGE_ENGS:
                item.deps.add(dep)
                dep.need_inc = True
            return
        item.deps.add(dep)
        if not dep.is_dma:
            dep.need_inc = True

    def add(self, eng, fn, reads=(), writes=(), dma=False, tag="", deps=()):
        it = Item(eng, fn, tag)
        if dma:
            it.is_dma = True
            pool = self.dma_pools[eng]
            ds = pool[self.dma_rr[eng] % len(pool)]
            self.dma_rr[eng] += 1
            if ds.uses:
                it.pre_waits = ((ds.sem, 16 * ds.uses),)
            ds.uses += 1
            it.dsem, it.dval = ds, 16 * ds.uses
        for d in deps:
            self._dep(it, d)
        for b in reads:
            self._dep(it, b.writer)
            b.readers.append(it)
        for b in writes:
            self._dep(it, b.writer)
            for r in b.readers:
                self._dep(it, r)
            b.readers = []
            b.writer = it
        if eng == "vector" and (self.vfence
                                or len(self.items[eng]) < FENCE_N):
            it.fence = True
        self.items[eng].append(it)
        return it

    def barrier(self, engines=("scalar", "vector", "tensor"), extra_deps=()):
        bsem = self._es.enter_context(self.nc.semaphore(f"bar{self._barn}"))
        self._barn += 1
        n = len(engines)
        for e in engines:
            it = self.add(e, lambda eng: eng.drain(), tag="bar")
            for d in extra_deps:
                self._dep(it, d)
            it.extra_inc = bsem
        for e in engines:
            w = Item(e, None, "barw")
            w.pre_waits = ((bsem, n),)
            self.items[e].append(w)
        self.last_barrier = bsem
        return bsem

    def finalize_and_emit(self):
        for e in ENGINES:
            c = 0
            for it in self.items[e]:
                if it.need_inc and not it.is_dma:
                    c += 1
                    it.sem_val = c
        sched = self
        with self.nc.Block() as block:
            def mk(e, items):
                def body(eng):
                    hwm = {}
                    for it in items:
                        ws = list(it.pre_waits)
                        for d in it.deps:
                            if d.is_dma:
                                ws.append((d.dsem.sem, d.dval))
                            else:
                                ws.append((sched.prog_sem[d.eng], d.sem_val))
                        for sem, val in ws:
                            k = id(sem)
                            if hwm.get(k, -1) >= val:
                                continue
                            hwm[k] = val
                            eng.wait_ge(sem, val)
                        inst = it.fn(eng) if it.fn is not None else None
                        if inst is None:
                            continue
                        if it.is_dma:
                            inst.then_inc(it.dsem.sem, 16)
                        elif it.extra_inc is not None:
                            inst.then_inc(it.extra_inc, 1)
                        elif it.need_inc:
                            inst.then_inc(sched.prog_sem[e], 1)
                return body
            for e in ENGINES:
                getattr(block, e)(mk(e, self.items[e]))


FFN_SUB = 99
import os as _os
SIM_EDGES = _os.environ.get("BASS_SIM_FP64") == "1"
EDGE_ENGS = set(
    e for e in _os.environ.get("MK_EDGE_ENGS", "").split(",") if e)
FENCE_N = int(_os.environ.get("MK_FENCE_N", "0"))


def build_program(dbg=(), stages=3):
    nc = bass.Bass("TRN2", target_bir_lowering=False, debug=False)
    es = ExitStack()

    def din(name, shape):
        return nc.dram_tensor(name, list(shape), F32, kind="ExternalInput").ap()

    x_h = din("x", (128, NK * L))
    inw_h = {d: din(f"inw_{d}", (32, 128, 1024)) for d in "fb"}
    outw_h = {d: din(f"outw_{d}", (NM, 128, NJ * 128)) for d in "fb"}
    mrgw_h = din("mrgw", (NM, 128, 16 * 128))
    xprj_h = {d: din(f"xprj_{d}", (128, NJ * 96)) for d in "fb"}
    dtw_h = {d: din(f"dtw_{d}", (64, DI)) for d in "fb"}
    alog_h = din("alog", (128, 2 * NJ * DS))
    dpar_h = din("dpar", (128, 2 * NJ))
    dtb_h = din("dtb", (128, 2 * NJ))
    cw_h = din("cw", (128, 2 * NJ * 4))
    cbp_h = din("cbp", (128, 2 * NJ))
    lng_h = din("lng", (128, NK))
    lnb_h = din("lnb", (128, NK))
    mrgb_h = din("mrgb", (128, NM))
    fw_h = din("fw", (NK, 128, 1024))
    fb_h = din("fb", (128, NK))
    fbr_h = din("fbr", (1, 1024))

    out_h = nc.dram_tensor("out", [128, NK * L], F32, kind="ExternalOutput").ap()
    xsp_h = nc.dram_tensor("xspill", [128, NK * L], F32, kind="Internal").ap()
    mrow_h = nc.dram_tensor("mrow", [1, 6144], F32, kind="Internal").ap()
    dbg_h = {name: nc.dram_tensor("dbg_" + name, list(shape), F32,
                                  kind="ExternalOutput").ap()
             for name, shape in dbg}

    def sb(name, shape, dt):
        return es.enter_context(nc.sbuf_tensor(name, list(shape), dt))

    X = sb("X", (128, NK * L), F32)
    XV = X.ap().bitcast(F16)
    XI0 = 16384
    OB0 = 24576
    XT = sb("XTxn2", (128, 16 * 1024), F16)
    XD = sb("XDoutf", (128, NK * L), F16)
    SC16 = sb("sc16", (128, 6784), F16)
    SC32 = sb("sc32", (128, 576), F32)
    WDF = sb("wdf", (128, NK * 1024), F16)
    NEGA = sb("negA", (128, 2 * NJ * DS), F32)
    SPAR = sb("spar", (128, 264), F32)
    EYE = sb("eye", (128, 128), F16)
    EYEN = sb("eyen", (128, 128), F16)
    IOP = sb("iop", (128, 1), F32)
    IOF = sb("iof", (128, 128), F32)
    W1024 = sb("w1024", (128, 1), F16)
    CAPS = sb("caps", (128, 2), F32)
    SROW32 = sb("srow32", (1, 1536), F32)
    STAT = sb("stat", (128, 256), F32)
    INWS = sb("inws", (128, 3 * 1024), F16)
    OUTWS = sb("outws", (128, 2 * 2048), F16)
    MRGS = sb("mrgs", (128, 2048), F16)
    ONER = sb("oner", (1, 512), F16)
    FBR = sb("fbrs", (1, 1024), F16)
    XPS = sb("xps", (128, NJ * 96), F16)
    DTS = sb("dts", (64, DI), F16)
    DGW = sb("dgw", (128, 8 * 128), F16)
    XST = sb("xst", (128, 2 * TC), F32)
    DGD = SC16[:, 2568:2696]
    A23 = sb("a23", (128, 512), F16)
    H2 = sb("h2", (128, 1024), F16)
    MUB = None

    PS = [es.enter_context(nc.psum_tensor(f"pb{i}", [128, 512], F32))
          for i in range(6)]
    PT16 = es.enter_context(nc.psum_tensor("pt16", [128, 1024], F16))
    PS7 = es.enter_context(nc.psum_tensor("pb7", [128, 512], F32))

    S = Sched(nc, es)
    bufs = {}

    def B(name):
        if name not in bufs:
            bufs[name] = Buf(name)
        return bufs[name]

    pbuf = [B(f"ps{i}") for i in range(6)]
    pt16b = B("pt16")
    ps7b = B("ps7")

    def dma(eng, out_ap, in_ap, reads=(), writes=(), tag="", deps=()):
        return S.add(eng, lambda E, o=out_ap, i=in_ap: E.dma_start(out=o, in_=i),
                     reads=reads, writes=writes, dma=True, tag=tag, deps=deps)

    def vop(fn, reads=(), writes=(), tag=""):
        return S.add("vector", fn, reads=reads, writes=writes, tag=tag)

    def sop(fn, reads=(), writes=(), tag=""):
        return S.add("scalar", fn, reads=reads, writes=writes, tag=tag)

    def top(fn, reads=(), writes=(), tag=""):
        return S.add("tensor", fn, reads=reads, writes=writes, tag=tag)

    dma("sync", X[:, :], x_h[:, :], writes=[B("X")], tag="ldx")
    dma("gpsimd", NEGA[:, :], alog_h[:, :], writes=[B("nega")])
    sop(lambda E: E.activation(out=NEGA[:, :], in_=NEGA[:, :], func=ACT.Exp),
        reads=[B("nega")], writes=[B("nega")])
    dma("gpsimd", SPAR[:, 0:32], dpar_h[:, :], writes=[B("spD")])
    dma("gpsimd", SPAR[:, 32:64], dtb_h[:, :], writes=[B("sp_ndtb")])
    sop(lambda E: E.mul(out=SPAR[:, 32:64], in_=SPAR[:, 32:64], mul=-1.0),
        reads=[B("sp_ndtb")], writes=[B("sp_ndtb")])
    dma("gpsimd", SPAR[:, 64:192], cw_h[:, :], writes=[B("spcw")])
    dma("gpsimd", SPAR[:, 192:200], lng_h[:, :], writes=[B("splng")])
    dma("gpsimd", SPAR[:, 200:208], lnb_h[:, :], writes=[B("splnb")])
    dma("gpsimd", SPAR[:, 208:216], mrgb_h[:, :], writes=[B("spmb")])
    dma("gpsimd", SPAR[:, 224:232], fb_h[:, :], writes=[B("spfb")])
    dma("gpsimd", FBR[:, :], fbr_h[:, :], writes=[B("fbr")])
    dma("gpsimd", SPAR[:, 232:264], cbp_h[:, :], writes=[B("spcb")])
    S.add("gpsimd", lambda E: E.iota(IOP[:, :], [[0, 1]], channel_multiplier=1,
                                     allow_small_or_imprecise_dtypes=True),
          writes=[B("iop")])
    S.add("gpsimd", lambda E: E.iota(IOF[:, :], [[1, 128]], channel_multiplier=0,
                                     allow_small_or_imprecise_dtypes=True),
          writes=[B("iof")])
    vop(lambda E: E.tensor_tensor(EYE[:, :], IOP[:, :].broadcast_to([128, 128]),
                                  IOF[:, :], ALU.is_equal),
        reads=[B("iop"), B("iof")], writes=[B("eye")])
    vop(lambda E: E.tensor_scalar(EYEN[:, :], EYE[:, :], -1.0, None, ALU.mult),
        reads=[B("eye")], writes=[B("eyen")])
    vop(lambda E: E.memset(W1024[:, :], 1.0 / 1024.0), writes=[B("w1024")])
    vop(lambda E: E.memset(ONER[:, :], 1.0), writes=[B("oner")])
    vop(lambda E: E.memset(CAPS[:, 0:1], 1e-5), writes=[B("caps")])
    vop(lambda E: E.memset(CAPS[:, 1:2], 1e-30), writes=[B("caps")])

    BF16 = mybir.dt.bfloat16
    I32 = mybir.dt.int32

    def grid_round(y32, y32b, W, out_ap, out_buf, fence_all=False):
        def gvop(*a, **k):
            it = vop(*a, **k)
            if fence_all:
                it.fence = True
            return it
        qa = SC16[:, 0:W]
        q = SC16[:, 1024:1024 + W]
        qb2 = SC16[:, 2048:2048 + W]
        m = SC16[:, 3072:3072 + W]
        ys = SC16[:, 4096:4096 + W]
        qb8 = SC16[:, 5120:5120 + W // 2].bitcast(F8)[:, 0:W]
        qab, qb_, qb2b = B("g_qa"), B("g_q"), B("g_qb2")
        mb, ysb = B("g_m"), B("g_ys")
        gvop(lambda E: E.tensor_copy(ys, y32), reads=[y32b], writes=[ysb])
        gvop(lambda E: E.tensor_scalar(y32.bitcast(I32), y32.bitcast(I32),
                                      0x7FFFFFFF, None, ALU.bitwise_and),
            reads=[y32b], writes=[y32b])
        gvop(lambda E: E.tensor_scalar(m, y32, 1.5, None, ALU.is_lt),
            reads=[y32b], writes=[mb])
        gvop(lambda E: E.tensor_scalar(qa.bitcast(BF16), y32, 16.0, None, ALU.add),
            reads=[y32b], writes=[qab])
        gvop(lambda E: E.tensor_scalar(qa, qa.bitcast(BF16), -16.0, None, ALU.add),
            reads=[qab], writes=[qab])
        gvop(lambda E: E.tensor_copy(qb8, y32), reads=[y32b], writes=[qb2b])
        gvop(lambda E: E.tensor_copy(qb2, qb8), reads=[qb2b], writes=[qb2b])
        gvop(lambda E: E.tensor_tensor(qa, qa, qb2, ALU.subtract),
            reads=[qab, qb2b], writes=[qab])
        gvop(lambda E: E.tensor_tensor(m, m, qa, ALU.mult),
            reads=[mb, qab], writes=[mb])
        gvop(lambda E: E.tensor_tensor(q, qb2, m, ALU.add),
            reads=[qb2b, mb], writes=[qb_])
        gvop(lambda E: E.tensor_scalar(ys.bitcast(I16), ys.bitcast(I16),
                                      -0x8000, None, ALU.bitwise_and),
            reads=[ysb], writes=[ysb])
        gvop(lambda E: E.tensor_tensor(out_ap.bitcast(I16), q.bitcast(I16),
                                      ys.bitcast(I16), ALU.bitwise_or),
            reads=[qb_, ysb], writes=[out_buf])

    XTF = XT.ap().bitcast(F32)
    XDSC = SC16[:, 5632:6656]

    S.vfence = True
    for ot in range(NK):
        wslot = XTF[:, (ot % 2) * 1024:(ot % 2) * 1024 + 1024]
        wb = B(f"w32{ot % 2}")
        dma("sync", wslot, fw_h[ot, :, :], writes=[wb], tag="fw")
        bm = SC32[:, 512:544]
        rb = SC32[:, 544:576]
        bmb, rbb = B("q_bm"), B("q_rb")
        vop(lambda E, wslot=wslot: E.tensor_reduce(
            bm, wslot.rearrange("p (g k) -> p g k", k=32),
            mybir.AxisListType.X, ALU.max, apply_absolute_value=True),
            reads=[wb], writes=[bmb])
        vop(lambda E: E.tensor_scalar(bm, bm, 7.5e-30, None, ALU.max),
            reads=[bmb], writes=[bmb])
        vop(lambda E: E.reciprocal(rb, bm), reads=[bmb], writes=[rbb])
        vop(lambda E: E.tensor_scalar(rb, rb, 7.5, None, ALU.mult),
            reads=[rbb], writes=[rbb])
        yw = XTF[:, 2048:3072]
        ywb = B("w_y")
        vop(lambda E, wslot=wslot: E.tensor_tensor(
            yw.rearrange("p (g k) -> p g k", k=32),
            wslot.rearrange("p (g k) -> p g k", k=32),
            rb.rearrange("p (g o) -> p g o", o=1).broadcast_to([128, 32, 32]),
            ALU.mult),
            reads=[wb, rbb], writes=[ywb])
        grid_round(yw, ywb, 1024, XDSC[:, 0:1024], B("g_out"))
        vop(lambda E: E.tensor_scalar(bm, bm, 1.0 / 7.5, None, ALU.mult),
            reads=[bmb], writes=[bmb])
        vop(lambda E, ot=ot: E.tensor_tensor(
            XD[:, ot * 1024:(ot + 1) * 1024].rearrange("p (g k) -> p g k", k=32),
            XDSC[:, 0:1024].rearrange("p (g k) -> p g k", k=32),
            bm.rearrange("p (g o) -> p g o", o=1).broadcast_to([128, 32, 32]),
            ALU.mult),
            reads=[B("g_out"), bmb], writes=[B(f"fwd{ot}")])
    for kt in range(NK):
        for ot in range(NK):
            _it = top(lambda E, kt=kt, ot=ot: E.transpose(
                PT16[:, ot * 128:(ot + 1) * 128],
                XD[:, ot * 1024 + kt * 128: ot * 1024 + (kt + 1) * 128], EYE[:, :]),
                reads=[B(f"fwd{ot}"), B("eye")], writes=[pt16b])
            _it.fence = True
        sop(lambda E, kt=kt: E.copy(out=WDF[:, kt * 1024:(kt + 1) * 1024],
                                    in_=PT16[:, :]),
            reads=[pt16b], writes=[B(f"wdf{kt}")])
    S.vfence = False

    MUB = XST.ap()

    def ffn_block(callid):
        S.vfence = True
        u = f"f{callid}"
        for kt in range(NK):
            guards = [B(f"fwd{kt * 2}"), B(f"fwd{kt * 2 + 1}")] if callid == 0 else []
            vop(lambda E, kt=kt: E.tensor_copy(
                XD[:, kt * L:(kt + 1) * L], X[:, kt * L:(kt + 1) * L]),
                reads=[B("X")], writes=[B(f"xb{kt}" + u)] + guards)
        def ffn_mms(c):
            for m in range(NM):
                bk = 2 + (c * NM + m) % 4
                PB = PS[bk]
                pb = pbuf[bk]
                top(lambda E, m=m, PB=PB: E.matmul(
                    PB[:, 0:512], FBR[:, m * 128:(m + 1) * 128], ONER[:, :],
                    start=True, stop=False),
                    reads=[B("fbr"), B("oner")], writes=[pb])
                for kt in range(NK):
                    top(lambda E, m=m, c=c, kt=kt, PB=PB: E.matmul(
                        PB[:, 0:512],
                        WDF[:, kt * 1024 + m * 128: kt * 1024 + (m + 1) * 128],
                        XD[:, kt * L + c * TC: kt * L + (c + 1) * TC],
                        start=False, stop=(kt == NK - 1)),
                        reads=[B(f"wdf{kt}"), B(f"xdf{kt}{c}" + u)],
                        writes=[pb])
                vop(lambda E, m=m, c=c, PB=PB: E.scalar_tensor_tensor(
                    X[:, m * L + c * TC: m * L + (c + 1) * TC],
                    PB[:, 0:512], 0.5,
                    X[:, m * L + c * TC: m * L + (c + 1) * TC],
                    ALU.mult, ALU.add),
                    reads=[pb, B("X")], writes=[B("X")])
        for c in range(NC_T):
            x2g = [B("w320"), B("w321"), B("w_y")] if (callid == 0 and c < 2) else []
            for kt in range(NK):
                sop(lambda E, kt=kt, c=c: E.activation(
                    out=XT[:, c * 4096 + kt * 512: c * 4096 + (kt + 1) * 512],
                    in_=X[:, kt * L + c * TC: kt * L + (c + 1) * TC],
                    func=ACT.Square),
                    reads=[B("X")], writes=[B(f"x2_{c}" + u)] + x2g)
            for kt in range(NK):
                top(lambda E, kt=kt, c=c: E.matmul(
                    PS7[0:1, 0:512], W1024[:, :],
                    XD[:, kt * L + c * TC: kt * L + (c + 1) * TC],
                    start=(kt == 0), stop=(kt == NK - 1)),
                    reads=[B(f"xb{kt}" + u), B("w1024")], writes=[ps7b])
            sop(lambda E: E.copy(out=SROW32[:, 0:512], in_=PS7[0:1, 0:512]),
                reads=[ps7b], writes=[B("murow")])
            for kt in range(NK):
                top(lambda E, kt=kt, c=c: E.matmul(
                    PS7[0:1, 0:512], W1024[:, :],
                    XT[:, c * 4096 + kt * 512: c * 4096 + (kt + 1) * 512],
                    start=(kt == 0), stop=(kt == NK - 1)),
                    reads=[B(f"x2_{c}" + u), B("w1024")], writes=[ps7b])
            sop(lambda E: E.activation(out=SROW32[:, 512:1024],
                                       in_=SROW32[:, 0:512], func=ACT.Square),
                reads=[B("murow")], writes=[B("varrow")])
            vop(lambda E: E.tensor_tensor(SROW32[:, 512:1024], PS7[0:1, 0:512],
                                          SROW32[:, 512:1024], ALU.subtract),
                reads=[ps7b, B("varrow")], writes=[B("varrow")])
            sop(lambda E: E.activation(out=SROW32[:, 512:1024],
                                       in_=SROW32[:, 512:1024], func=ACT.Sqrt,
                                       bias=CAPS[0:1, 0:1]),
                reads=[B("varrow"), B("caps")], writes=[B("varrow")])
            vop(lambda E: E.reciprocal(SROW32[:, 512:1024], SROW32[:, 512:1024]),
                reads=[B("varrow")], writes=[B("varrow")])
            dma("sync", mrow_h[:, 0:1024], SROW32[:, 0:1024],
                reads=[B("murow"), B("varrow")], writes=[B("mrowh")], tag="rowst")
            dma("sync", MUB[:, 0:512], mrow_h[0:1, 0:512].partition_broadcast(128),
                reads=[B("mrowh")], writes=[B("mub")], tag="rowbc")
            dma("sync", MUB[:, 512:1024],
                mrow_h[0:1, 512:1024].partition_broadcast(128),
                reads=[B("mrowh")], writes=[B("rsb")], tag="rowbc")
            fo = c * 2048
            A = XTF[:, fo:fo + 512]
            MS = XTF[:, fo + 512:fo + 1024]
            RS = XTF[:, fo + 1024:fo + 1536]
            D = XTF[:, fo + 1536:fo + 2048]
            Ab, MSb, RSb, Db = B("scrA"), B("scrMS"), B("scrRS"), B("scrD")
            _it = vop(lambda E, MS=MS: E.transpose(MS, MUB[:, 0:512]),
                reads=[B("mub")], writes=[MSb, B(f"x2_{c}" + u)])
            _it.fence = True
            _it = vop(lambda E, RS=RS: E.transpose(RS, MUB[:, 512:1024]),
                reads=[B("rsb")], writes=[RSb])
            _it.fence = True
            bm = SC32[:, 512:528]
            rb = SC32[:, 528:544]
            sc = SC32[:, 544:560]
            bmb, rbb, scb = B("x_bm"), B("x_rb"), B("x_sc")
            for kt in range(NK):
                _it = vop(lambda E, kt=kt, c=c, A=A: E.transpose(
                    A, X[:, kt * L + c * TC: kt * L + (c + 1) * TC]),
                    reads=[B("X")], writes=[Ab])
                _it.fence = True
                vop(lambda E, A=A, MS=MS: E.tensor_tensor(A, A, MS, ALU.subtract),
                    reads=[Ab, MSb], writes=[Ab])
                _it = vop(lambda E, A=A: E.tensor_reduce(
                    bm, A.rearrange("p (g k) -> p g k", k=32),
                    mybir.AxisListType.X, ALU.max, apply_absolute_value=True),
                    reads=[Ab], writes=[bmb])
                _it.fence = True
                vop(lambda E: E.tensor_scalar(bm, bm, 7.5e-30, None, ALU.max),
                    reads=[bmb], writes=[bmb])
                _it = vop(lambda E: E.reciprocal(rb, bm), reads=[bmb], writes=[rbb])
                _it.fence = True
                vop(lambda E: E.tensor_scalar(rb, rb, 7.5, None, ALU.mult),
                    reads=[rbb], writes=[rbb])
                vop(lambda E, A=A, D=D: E.tensor_tensor(
                    D.rearrange("p (g k) -> p g k", k=32),
                    A.rearrange("p (g k) -> p g k", k=32),
                    rb.rearrange("p (g o) -> p g o", o=1).broadcast_to([128, 16, 32]),
                    ALU.mult),
                    reads=[Ab, rbb], writes=[Db])
                grid_round(D, Db, 512, XDSC[:, 0:512], B("g_out"),
                           fence_all=_os.environ.get("MK_GRF", "") == "1")
                vop(lambda E, RS=RS: E.tensor_tensor(
                    sc, bm, RS[:, 0:512:32], ALU.mult),
                    reads=[bmb, RSb], writes=[scb])
                vop(lambda E: E.tensor_scalar(sc, sc, 1.0 / 7.5, None, ALU.mult),
                    reads=[scb], writes=[scb])
                vop(lambda E: E.tensor_tensor(
                    XDSC[:, 0:512].rearrange("p (g k) -> p g k", k=32),
                    XDSC[:, 0:512].rearrange("p (g k) -> p g k", k=32),
                    sc.rearrange("p (g o) -> p g o", o=1).broadcast_to([128, 16, 32]),
                    ALU.mult),
                    reads=[B("g_out"), scb], writes=[B("g_out")])
                _it = vop(lambda E, kt=kt, c=c: E.transpose(
                    XD[:, kt * L + c * TC: kt * L + (c + 1) * TC], XDSC[:, 0:512]),
                    reads=[B("g_out")], writes=[B(f"xdf{kt}{c}" + u), B(f"xb{kt}" + u)])
                _it.fence = True
            ffn_mms(c)
        S.vfence = False


    def mixer():
        for kt in range(NK):
            vop(lambda E, kt=kt: E.tensor_copy(
                XT[:, kt * L:(kt + 1) * L], X[:, kt * L:(kt + 1) * L]),
                reads=[B("X")], writes=[B(f"mxb{kt}")])
            sop(lambda E, kt=kt: E.activation(
                out=XD[:, kt * L:(kt + 1) * L], in_=X[:, kt * L:(kt + 1) * L],
                func=ACT.Square),
                reads=[B("X")], writes=[B("mx2")])
        for c in range(NC_T):
            for kt in range(NK):
                top(lambda E, c=c, kt=kt: E.matmul(
                    PS7[0:1, 0:512], W1024[:, :],
                    XT[:, kt * L + c * TC: kt * L + c * TC + TC],
                    start=(kt == 0), stop=(kt == NK - 1)),
                    reads=[B(f"mxb{kt}"), B("w1024")], writes=[ps7b])
            sop(lambda E: E.copy(out=SROW32[:, 0:512], in_=PS7[0:1, 0:512]),
                reads=[ps7b], writes=[B("murow")])
            for kt in range(NK):
                top(lambda E, c=c, kt=kt: E.matmul(
                    PS7[0:1, 0:512], W1024[:, :],
                    XD[:, kt * L + c * TC: kt * L + c * TC + TC],
                    start=(kt == 0), stop=(kt == NK - 1)),
                    reads=[B("mx2"), B("w1024")], writes=[ps7b])
            VROW = SC32[0:1, 0:512]
            sop(lambda E, VROW=VROW: E.activation(out=VROW,
                                       in_=SROW32[:, 0:512], func=ACT.Square),
                reads=[B("murow")], writes=[B("varrow"), B("sig")])
            vop(lambda E, VROW=VROW: E.tensor_tensor(VROW, PS7[0:1, 0:512],
                                          VROW, ALU.subtract),
                reads=[ps7b, B("varrow")], writes=[B("varrow"), B("sig")])
            sop(lambda E, VROW=VROW: E.activation(out=SROW32[:, 512:1024],
                                       in_=VROW, func=ACT.Sqrt,
                                       bias=CAPS[0:1, 0:1]),
                reads=[B("varrow"), B("caps")], writes=[B("rstd1")])
            vop(lambda E: E.reciprocal(SROW32[:, 512:1024], SROW32[:, 512:1024]),
                reads=[B("rstd1")], writes=[B("rstd1")])
            sop(lambda E: E.activation(out=SROW32[:, 1024:1536],
                                       in_=SROW32[:, 512:1024], func=ACT.Square),
                reads=[B("rstd1")], writes=[B("reff")])
            vop(lambda E, VROW=VROW: E.tensor_tensor(SROW32[:, 1024:1536],
                                          VROW,
                                          SROW32[:, 1024:1536], ALU.mult),
                reads=[B("varrow"), B("reff")], writes=[B("reff")])
            sop(lambda E: E.activation(out=SROW32[:, 1024:1536],
                                       in_=SROW32[:, 1024:1536], func=ACT.Sqrt,
                                       bias=CAPS[0:1, 0:1]),
                reads=[B("reff"), B("caps")], writes=[B("reff")])
            vop(lambda E: E.reciprocal(SROW32[:, 1024:1536], SROW32[:, 1024:1536]),
                reads=[B("reff")], writes=[B("reff")])
            vop(lambda E: E.tensor_tensor(SROW32[:, 1024:1536], SROW32[:, 512:1024],
                                          SROW32[:, 1024:1536], ALU.mult),
                reads=[B("rstd1"), B("reff")], writes=[B("reff")])
            dma("sync", mrow_h[:, c * 1536:(c + 1) * 1536], SROW32[:, 0:1536],
                reads=[B("murow"), B("rstd1"), B("reff")], writes=[B("mrowh")],
                tag="rowst")
            dma("sync", MUB[:, 0:512],
                mrow_h[0:1, c * 1536:c * 1536 + 512].partition_broadcast(128),
                reads=[B("mrowh")], writes=[B("mub")], tag="rowbc")
            dma("sync", MUB[:, 512:1024],
                mrow_h[0:1, c * 1536 + 1024:c * 1536 + 1536]
                .partition_broadcast(128),
                reads=[B("mrowh")], writes=[B("rsb")], tag="rowbc")
            for kt in range(NK):
                vop(lambda E, kt=kt, c=c: E.tensor_tensor(
                    SC16[:, 0:TC],
                    X[:, kt * L + c * TC: kt * L + (c + 1) * TC],
                    MUB[:, 0:512], ALU.subtract),
                    reads=[B("X"), B("mub")], writes=[B("xn2t")])
                vop(lambda E: E.tensor_tensor(
                    SC16[:, 0:TC], SC16[:, 0:TC], MUB[:, 512:1024], ALU.mult),
                    reads=[B("xn2t"), B("rsb")], writes=[B("xn2t")])
                vop(lambda E, kt=kt, c=c: E.tensor_scalar(
                    XT[:, kt * L + c * TC: kt * L + (c + 1) * TC],
                    SC16[:, 0:TC], SPAR[:, 192 + kt:193 + kt],
                    SPAR[:, 200 + kt:201 + kt], ALU.mult, ALU.add),
                    reads=[B("xn2t"), B("splng"), B("splnb")],
                    writes=[B(f"xn2_{kt}"), B(f"mxb{kt}")])
        spills = []
        for kt in range(NK):
            spills.append(dma("sync", xsp_h[:, kt * L:(kt + 1) * L],
                              X[:, kt * L:(kt + 1) * L], reads=[B("X")],
                              writes=[B(f"xsp{kt}")], tag="xspill"))
        S.barrier(extra_deps=spills)
        for di, dname in enumerate("fb"):
            mixer_dir(di, dname)
        bar = S.barrier()
        return bar

    def mixer_dir(di, dname):
        rev = (dname == "b")
        dma("gpsimd", XPS[:, :], xprj_h[dname][:, :], writes=[B("xps")])
        dma("gpsimd", DTS[:, :], dtw_h[dname][:, :], writes=[B("dts")])
        vop(lambda E: E.memset(STAT[:, :], 0.0),
            writes=[B(f"stat{q}") for q in range(DS)])
        chunks = list(range(NC_T))
        if rev:
            chunks = chunks[::-1]
        for ci, c in enumerate(chunks):
            mixer_chunk(di, dname, c, rev, first_c=(ci == 0))

    def mixer_chunk(di, dname, c, rev, first_c):
        u = f"{dname}{c}"
        t0 = c * TC
        PZ = PT16.ap().bitcast(F32)
        proj = SC16[:, 6144:6656]
        sz = SC16[:, 1544:2056]
        for j in range(NJ):
            p = j % 2
            ext = SC16[:, 512 + p * 516:1028 + p * 516]
            extb = B(f"ext{p}")
            tb = B(f"tail{j}")
            sl = INWS[:, p * 1024:p * 1024 + 1024]
            slb = B(f"inw{p}")
            ip_ap = PS[0][:, 0:512] if p == 0 else PZ[:, 0:512]
            ipb = pbuf[0] if p == 0 else pt16b
            cv_ap = PS[1][:, 0:512] if p == 0 else PS[5][:, 0:512]
            cvb = pbuf[1] if p == 0 else pbuf[5]
            dma("gpsimd", sl, inw_h[dname][j, :, :], writes=[slb], tag="inw")
            for kt in range(NK):
                top(lambda E, sl=sl, kt=kt, t0=t0, ip_ap=ip_ap: E.matmul(
                    ip_ap, sl[:, kt * 128:(kt + 1) * 128],
                    XT[:, kt * L + t0: kt * L + t0 + TC],
                    start=(kt == 0), stop=(kt == NK - 1)),
                    reads=[slb, B(f"xn2_{kt}")], writes=[ipb])
            for tap in range(4):
                vop(lambda E, tap=tap, j=j, p=p: E.tensor_scalar(
                    DGW[:, (p * 4 + tap) * 128:(p * 4 + tap + 1) * 128], EYE[:, :],
                    SPAR[:, 64 + di * 64 + j * 4 + tap:65 + di * 64 + j * 4 + tap],
                    None, ALU.mult),
                    reads=[B("eye"), B("spcw")], writes=[B(f"dgw{p}{tap}")])
            if not rev:
                sop(lambda E, ext=ext, ip_ap=ip_ap: E.copy(out=ext[:, 4:516],
                                              in_=ip_ap),
                    reads=[ipb], writes=[extb])
                if first_c:
                    sop(lambda E, ext=ext: E.memzero(ext[:, 0:4]),
                        writes=[extb])
                else:
                    sop(lambda E, ext=ext, j=j: E.copy(
                        out=ext[:, 0:4], in_=SC16[:, 6656 + j * 4:6660 + j * 4]),
                        reads=[tb], writes=[extb])
                sop(lambda E, ext=ext, j=j: E.copy(
                    out=SC16[:, 6656 + j * 4:6660 + j * 4], in_=ext[:, 512:516]),
                    reads=[extb], writes=[tb])
                taps = [ext[:, tap + 1: tap + 513] for tap in range(4)]
            else:
                sop(lambda E, ext=ext, ip_ap=ip_ap: E.copy(out=ext[:, 0:512],
                                              in_=ip_ap),
                    reads=[ipb], writes=[extb])
                if first_c:
                    sop(lambda E, ext=ext: E.memzero(ext[:, 512:516]),
                        writes=[extb])
                else:
                    sop(lambda E, ext=ext, j=j: E.copy(
                        out=ext[:, 512:516],
                        in_=SC16[:, 6656 + j * 4:6660 + j * 4]),
                        reads=[tb], writes=[extb])
                sop(lambda E, ext=ext, j=j: E.copy(
                    out=SC16[:, 6656 + j * 4:6660 + j * 4], in_=ext[:, 0:4]),
                    reads=[extb], writes=[tb])
                taps = [ext[:, 3 - tap: 515 - tap] for tap in range(4)]
            for tap in range(4):
                top(lambda E, tap=tap, ap=taps[tap], p=p, cv_ap=cv_ap: E.matmul(
                    cv_ap,
                    DGW[:, (p * 4 + tap) * 128:(p * 4 + tap + 1) * 128], ap,
                    start=(tap == 0), stop=(tap == 3)),
                    reads=[B(f"dgw{p}{tap}"), extb], writes=[cvb])
            sop(lambda E, j=j, cv_ap=cv_ap: E.activation(
                out=XV[:, XI0 + j * TC: XI0 + (j + 1) * TC], in_=cv_ap,
                func=ACT.Silu,
                bias=SPAR[:, 232 + di * NJ + j:233 + di * NJ + j]),
                reads=[cvb, B("spcb")], writes=[B(f"xi{j}")])
            top(lambda E, j=j: E.matmul(
                PS[2][0:96, 0:512], XPS[:, j * 96:(j + 1) * 96],
                XV[:, XI0 + j * TC: XI0 + (j + 1) * TC],
                start=(j == 0), stop=(j == NJ - 1)),
                reads=[B("xps"), B(f"xi{j}")], writes=[pbuf[2]])
        sop(lambda E: E.copy(out=proj[0:96, 0:512], in_=PS[2][0:96, 0:512]),
            reads=[pbuf[2]], writes=[B("proj")])
        for s in range(DS):
            top(lambda E, s=s: E.matmul(
                PS[3][:, 0:512],
                EYEN[0:96, 64 + s:65 + s].broadcast_to([96, 128]),
                proj[0:96, 0:512], start=True, stop=True),
                reads=[B("eyen"), B("proj")], writes=[pbuf[3]])
            sop(lambda E, s=s: E.copy(out=XV[:, s * TC:(s + 1) * TC],
                                      in_=PS[3][:, 0:512]),
                reads=[pbuf[3]], writes=[B(f"brep{s}")])
            top(lambda E, s=s: E.matmul(
                PS[0][:, 0:512],
                EYE[0:96, 80 + s:81 + s].broadcast_to([96, 128]),
                proj[0:96, 0:512], start=True, stop=True),
                reads=[B("eye"), B("proj")], writes=[pbuf[0]])
            sop(lambda E, s=s: E.copy(out=XV[:, (16 + s) * TC:(17 + s) * TC],
                                      in_=PS[0][:, 0:512]),
                reads=[pbuf[0]], writes=[B(f"crep{s}")])
        lns_s = [SC16[:, 0:512], XV[:, 31744:32256]]
        u_s = [SC16[:, 2056:2568], XV[:, 32256:32768]]
        b_s4 = [SC16[:, 4096 + q * 512:4608 + q * 512] for q in range(4)]
        h_aps = [H2[:, 0:512], H2[:, 512:1024]]
        a_aps = [SC16[:, 3072:3584], SC16[:, 3584:4096], A23[:, 0:512]]

        def l2_pro(j):
            p = j % 2
            top(lambda E, j=j: E.matmul(
                PS[4][:, 0:512], DTS[0:64, j * 128:(j + 1) * 128],
                proj[0:64, 0:512], start=True, stop=True),
                reads=[B("dts"), B("proj")], writes=[pbuf[4]])
            sop(lambda E, j=j: E.activation(
                out=SC32[:, 0:512], in_=PS[4][:, 0:512], func=ACT.Sigmoid,
                scale=-1.0,
                bias=SPAR[:, 32 + di * NJ + j:33 + di * NJ + j]),
                reads=[pbuf[4], B("sp_ndtb")], writes=[B("sig")])
            sop(lambda E, p=p: E.activation(out=lns_s[p], in_=SC32[:, 0:512],
                                            func=ACT.Ln, bias=CAPS[:, 1:2]),
                reads=[B("sig"), B("caps")], writes=[B(f"lns{p}")])

        def l2_u(j):
            p = j % 2
            vop(lambda E, j=j, p=p: E.tensor_tensor(
                u_s[p], lns_s[p], XV[:, XI0 + j * TC: XI0 + (j + 1) * TC],
                ALU.mult),
                reads=[B(f"lns{p}"), B(f"xi{j}")], writes=[B(f"uu{p}")])

        l2_pro(0)
        l2_u(0)
        for j in range(NJ):
            p = j % 2
            if j + 1 < NJ:
                l2_pro(j + 1)

            def mk_sz():
                sop(lambda E: E.activation(out=sz, in_=PZ[:, 0:512],
                                           func=ACT.Silu),
                    reads=[pt16b], writes=[B("sz")])
            lnsj, uj = lns_s[p], u_s[p]
            lnsb, ub = B(f"lns{p}"), B(f"uu{p}")
            slz = INWS[:, 2048:3072]
            slzb = B("inwz")
            dma("gpsimd", slz, inw_h[dname][16 + j, :, :], writes=[slzb],
                tag="inwz")
            for kt in range(NK):
                top(lambda E, slz=slz, kt=kt, t0=t0: E.matmul(
                    PZ[:, 0:512], slz[:, kt * 128:(kt + 1) * 128],
                    XT[:, kt * L + t0: kt * L + t0 + TC],
                    start=(kt == 0), stop=(kt == NK - 1)),
                    reads=[slzb, B(f"xn2_{kt}")], writes=[pt16b])
            vop(lambda E, j=j: E.tensor_scalar(
                DGD[:, :], EYE[:, :], SPAR[:, di * NJ + j:di * NJ + j + 1],
                None, ALU.mult),
                reads=[B("eye"), B("spD")], writes=[B("dgd")])
            top(lambda E, j=j: E.matmul(
                PS[5][:, 0:512], DGD[:, :],
                XV[:, XI0 + j * TC: XI0 + (j + 1) * TC],
                start=True, stop=False),
                reads=[B("dgd"), B(f"xi{j}")], writes=[pbuf[5]])
            def scopy(s):
                hp = s % 2
                h_ap = h_aps[hp]
                st = STAT[:, j * 16 + s:j * 16 + s + 1]
                src = h_ap[:, 0:1] if rev else h_ap[:, TC - 1:TC]
                sop(lambda E, st=st, src=src: E.copy(out=st, in_=src),
                    reads=[B(f"h{hp}")], writes=[B(f"stat{s}")])

            def mk_aexp(jj, s):
                sp = (jj + s) % 3
                lp = jj % 2
                sop(lambda E, jj=jj, s=s, sp=sp, lp=lp: E.activation(
                    out=a_aps[sp], in_=lns_s[lp], func=ACT.Exp,
                    scale=NEGA[:, di * 256 + jj * 16 + s:
                               di * 256 + jj * 16 + s + 1]),
                    reads=[B(f"lns{lp}"), B("nega")], writes=[B(f"a{sp}")])

            for s in range(DS):
                sp = (j + s) % 3
                hp = s % 2
                s4 = s % 4
                a_ap = a_aps[sp]
                h_ap = h_aps[hp]
                b_ap = b_s4[s4]
                ab, bb, hb = B(f"a{sp}"), B(f"b{s4}"), B(f"h{hp}")
                if j == 0 or s >= 3:
                    mk_aexp(j, s)
                if s >= 2:
                    scopy(s - 2)
                if s == 5:
                    mk_sz()
                if s % 2 == 0:
                    vop(lambda E, s=s, s4=s4, uj=uj: E.tensor_tensor(
                        SC16[:, 4096 + s4 * 512:4096 + s4 * 512 + 1024]
                        .rearrange("p (o t) -> p o t", o=2),
                        uj.rearrange("p (o t) -> p o t", o=1)
                        .broadcast_to([128, 2, 512]),
                        XV[:, s * TC:(s + 2) * TC]
                        .rearrange("p (o t) -> p o t", o=2),
                        ALU.mult),
                        reads=[ub, B(f"brep{s}"), B(f"brep{s + 1}")],
                        writes=[bb, B(f"b{s4 + 1}")])
                st = STAT[:, j * 16 + s:j * 16 + s + 1]
                if rev:
                    vop(lambda E, a_ap=a_ap, b_ap=b_ap, h_ap=h_ap, st=st:
                        E.tensor_tensor_scan(h_ap[:, ::-1], a_ap[:, ::-1],
                                             b_ap[:, ::-1], st, ALU.mult,
                                             ALU.add),
                        reads=[ab, bb, B(f"stat{s}")], writes=[hb])
                else:
                    vop(lambda E, a_ap=a_ap, b_ap=b_ap, h_ap=h_ap, st=st:
                        E.tensor_tensor_scan(h_ap, a_ap, b_ap, st, ALU.mult,
                                             ALU.add),
                        reads=[ab, bb, B(f"stat{s}")], writes=[hb])
                if s % 2 == 1:
                    q0 = (s - 1) % 4
                    vop(lambda E, s=s, q0=q0: E.tensor_tensor(
                        SC16[:, 4096 + q0 * 512:4096 + q0 * 512 + 1024],
                        H2[:, 0:1024],
                        XV[:, (15 + s) * TC:(17 + s) * TC], ALU.mult),
                        reads=[B("h0"), B("h1"),
                               B(f"crep{s - 1}"), B(f"crep{s}")],
                        writes=[B(f"b{q0}"), B(f"b{q0 + 1}")])
                    for sq in (s - 1, s):
                        top(lambda E, sq=sq: E.matmul(
                            PS[5][:, 0:512], EYE[:, :], b_s4[sq % 4],
                            start=False, stop=(sq == DS - 1)),
                            reads=[B("eye"), B(f"b{sq % 4}")], writes=[pbuf[5]])
            if j + 1 < NJ:
                l2_u(j + 1)
            scopy(DS - 2)
            if j + 1 < NJ:
                for s_pre in range(3):
                    mk_aexp(j + 1, s_pre)
            scopy(DS - 1)
            vop(lambda E, j=j: E.tensor_tensor(
                XV[:, XI0 + j * TC: XI0 + (j + 1) * TC], PS[5][:, 0:512], sz,
                ALU.mult),
                reads=[pbuf[5], B("sz")], writes=[B(f"xi{j}")])
        for m in range(NM):
            slo = OUTWS[:, (m % 2) * 2048:(m % 2) * 2048 + 2048]
            slob = B(f"outw{m % 2}")
            dma("gpsimd", slo, outw_h[dname][m, :, :], writes=[slob], tag="outw")
            op_ap = PS7[:, 0:512] if m % 2 == 0 else PS[4][:, 0:512]
            opb = ps7b if m % 2 == 0 else pbuf[4]
            for j in range(NJ):
                top(lambda E, slo=slo, j=j, op_ap=op_ap: E.matmul(
                    op_ap, slo[:, j * 128:(j + 1) * 128],
                    XV[:, XI0 + j * TC: XI0 + (j + 1) * TC],
                    start=(j == 0), stop=(j == NJ - 1)),
                    reads=[slob, B(f"xi{j}")], writes=[opb])
            if di == 0:
                sop(lambda E, m=m, t0=t0, op_ap=op_ap: E.copy(
                    out=XD[:, m * L + t0: m * L + t0 + TC], in_=op_ap),
                    reads=[opb],
                    writes=[B(f"outf{m}{c}"), B("mx2")])
            else:
                sop(lambda E, m=m, op_ap=op_ap: E.copy(
                    out=XV[:, OB0 + m * TC: OB0 + (m + 1) * TC],
                    in_=op_ap),
                    reads=[opb], writes=[B(f"outb{m}")])
        if di == 1:
            XF = X.ap()
            mbm = XF[:, 14336:14848]
            rbm = XF[:, 14848:15360]
            dma("sync", mbm,
                mrow_h[0:1, c * 1536:c * 1536 + 512].partition_broadcast(128),
                reads=[B("mrowh")], writes=[B("mubm"), B("X")], tag="mrowbc")
            dma("sync", rbm,
                mrow_h[0:1, c * 1536 + 512:c * 1536 + 1024]
                .partition_broadcast(128),
                reads=[B("mrowh")], writes=[B("rsbm")], tag="mrowbc")
            for m in range(NM):
                slm = MRGS[:, 0:2048]
                slmb = B("mrg0")
                dma("gpsimd", slm, mrgw_h[m, :, :], writes=[slmb], tag="mrgw")
                xs = XST[:, (m % 2) * TC:(m % 2) * TC + TC]
                xsb = B("mub") if m % 2 == 0 else B("rsb")
                dma("sync", xs, xsp_h[:, m * L + t0: m * L + t0 + TC],
                    reads=[B(f"xsp{m}")], writes=[xsb], tag="xrld")
                mg_ap = PS[3][:, 0:512] if m % 2 == 0 else PS7[:, 0:512]
                mgb = pbuf[3] if m % 2 == 0 else ps7b
                for k in range(NK):
                    top(lambda E, slm=slm, k=k, t0=t0, mg_ap=mg_ap: E.matmul(
                        mg_ap, slm[:, k * 128:(k + 1) * 128],
                        XD[:, k * L + t0: k * L + t0 + TC],
                        start=(k == 0), stop=False),
                        reads=[slmb, B(f"outf{k}{c}")], writes=[mgb])
                for k in range(NK):
                    top(lambda E, slm=slm, k=k, mg_ap=mg_ap: E.matmul(
                        mg_ap, slm[:, (8 + k) * 128:(9 + k) * 128],
                        XV[:, OB0 + k * TC: OB0 + (k + 1) * TC],
                        start=False, stop=(k == NK - 1)),
                        reads=[slmb, B(f"outb{k}")], writes=[mgb])
                t1 = XF[:, 15360:15872]
                t1b = B("mg_t1")
                vop(lambda E, xs=xs, t1=t1: E.tensor_tensor(
                    t1, xs, mbm, ALU.subtract),
                    reads=[xsb, B("mubm")], writes=[t1b])
                vop(lambda E, t1=t1: E.tensor_tensor(t1, t1, rbm, ALU.mult),
                    reads=[t1b, B("rsbm")], writes=[t1b])
                vop(lambda E, xs=xs, m=m, mg_ap=mg_ap: E.scalar_tensor_tensor(
                    xs, mg_ap, SPAR[:, 208 + m:209 + m], xs,
                    ALU.add, ALU.add),
                    reads=[mgb, xsb, B("spmb")], writes=[xsb])
                vop(lambda E, xs=xs, t1=t1: E.tensor_tensor(xs, xs, t1, ALU.add),
                    reads=[xsb, t1b], writes=[xsb])
                dma("sync", xsp_h[:, m * L + t0: m * L + t0 + TC], xs,
                    reads=[xsb], writes=[B(f"xsp{m}")], tag="xwr")

    sm = stages if stages > 4 else {0: 0, 1: 1, 2: 3, 3: 7}.get(stages, 7)
    if sm & 1:
        ffn_block(0)
        S.barrier()
    if sm & 2:
        bar = mixer()
        for kt in range(NK):
            it = dma("sync", X[:, kt * L:(kt + 1) * L],
                     xsp_h[:, kt * L:(kt + 1) * L],
                     reads=[B(f"xsp{kt}")], writes=[B("X")], tag="xrld2")
            it.pre_waits = tuple(it.pre_waits) + ((bar, 3),)
    if sm & 4:
        ffn_block(1)
    st_items = []
    for kt in range(NK):
        st_items.append(dma("sync", out_h[:, kt * L:(kt + 1) * L],
                            X[:, kt * L:(kt + 1) * L], reads=[B("X")], tag="st"))
    fin = Item("gpsimd", None, "fin")
    for it in st_items:
        fin.deps.add(it)
    S.items["gpsimd"].append(fin)

    S.finalize_and_emit()
    es.close()
    return nc



def _tile_fm(a):
    D, T = a.shape
    return np.ascontiguousarray(
        a.reshape(D // 128, 128, T).transpose(1, 0, 2).reshape(128, -1))


def _untile_fm(a, D, T):
    return np.ascontiguousarray(
        a.reshape(128, D // 128, T).transpose(1, 0, 2).reshape(D, T))


def _tile_lhsT(wT):
    K, M = wT.shape
    nk, nm = K // 128, M // 128
    a = wT.reshape(nk, 128, nm, 128).transpose(2, 1, 0, 3)
    return np.ascontiguousarray(a.reshape(nm, 128, nk * 128))


def _col_pack(v):
    return np.ascontiguousarray(v.reshape(-1, 128).T)


_NC_CACHE = {}


def _get_nc():
    import os
    global FFN_SUB
    FFN_SUB = int(os.environ.get("MK_FFN_SUB", "99"))
    st = int(os.environ.get("MK_STAGES", "3"))
    key = (st, FFN_SUB)
    if key not in _NC_CACHE:
        _NC_CACHE[key] = build_program(stages=st)
    return _NC_CACHE[key]


def kernel(**inp):
    f = {k: np.ascontiguousarray(np.asarray(v, dtype=np.float32))
         for k, v in inp.items()}
    x = f["L0_attn_x"]
    nb = x.shape[0]

    shared = {}
    for d, pre in (("f", "L0_attn_fwd_"), ("b", "L0_attn_bwd_")):
        shared[f"inw_{d}"] = _tile_lhsT(np.ascontiguousarray(f[pre + "in_w"].T))
        shared[f"outw_{d}"] = _tile_lhsT(np.ascontiguousarray(f[pre + "out_w"].T))
        wT = np.ascontiguousarray(f[pre + "xproj_w"].T)
        shared[f"xprj_{d}"] = np.ascontiguousarray(
            wT.reshape(NJ, 128, 96).transpose(1, 0, 2).reshape(128, NJ * 96))
        shared[f"dtw_{d}"] = np.ascontiguousarray(f[pre + "dt_w"].T)
    shared["mrgw"] = _tile_lhsT(np.ascontiguousarray(f["L0_attn_merge_w"].T))

    def pack2(vf, vb):
        return np.ascontiguousarray(
            np.concatenate([_col_pack(vf), _col_pack(vb)], axis=1))

    shared["alog"] = np.ascontiguousarray(np.concatenate(
        [f["L0_attn_fwd_A_log"].reshape(NJ, 128, DS).transpose(1, 0, 2)
         .reshape(128, NJ * DS),
         f["L0_attn_bwd_A_log"].reshape(NJ, 128, DS).transpose(1, 0, 2)
         .reshape(128, NJ * DS)], axis=1))
    shared["dpar"] = pack2(f["L0_attn_fwd_D"], f["L0_attn_bwd_D"])
    shared["dtb"] = pack2(f["L0_attn_fwd_dt_b"], f["L0_attn_bwd_dt_b"])
    shared["cw"] = np.ascontiguousarray(np.concatenate(
        [f["L0_attn_fwd_conv_w"].reshape(NJ, 128, 4).transpose(1, 0, 2)
         .reshape(128, NJ * 4),
         f["L0_attn_bwd_conv_w"].reshape(NJ, 128, 4).transpose(1, 0, 2)
         .reshape(128, NJ * 4)], axis=1))
    shared["cbp"] = pack2(f["L0_attn_fwd_conv_b"], f["L0_attn_bwd_conv_b"])
    shared["lng"] = _col_pack(f["L0_attn_ln_g"])
    shared["lnb"] = _col_pack(f["L0_attn_ln_b"])
    shared["mrgb"] = _col_pack(f["L0_attn_merge_b"])
    shared["fw"] = np.ascontiguousarray(f["L0_ffn_weight"].reshape(NK, 128, 1024))
    shared["fb"] = _col_pack(f["L0_ffn_bias"])
    shared["fbr"] = np.ascontiguousarray(f["L0_ffn_bias"][None, :])

    in_maps = []
    for b in range(nb):
        m = dict(shared)
        m["x"] = _tile_fm(np.ascontiguousarray(x[b].T))
        in_maps.append(m)

    nc = _get_nc()
    res = run_bass_kernel_spmd(nc, in_maps, core_ids=list(range(8)))
    outs = []
    for b in range(nb):
        o = _untile_fm(res.results[b]["out"], DM, L)
        outs.append(o.T)
    return np.stack(outs, axis=0).astype(np.float32)


# version 18
# speedup vs baseline: 1.0054x; 1.0054x over previous
import numpy as np
from contextlib import ExitStack

import concourse.bass as bass
import concourse.mybir as mybir
from concourse.bass_utils import run_bass_kernel_spmd

F32 = mybir.dt.float32
F16 = mybir.dt.float16
F8 = mybir.dt.float8e4
I16 = mybir.dt.int16
ALU = mybir.AluOpType
ACT = mybir.ActivationFunctionType

DM = 1024
DI = 2048
L = 2048
DS = 16
NK = DM // 128
NJ = DI // 128
NM = DM // 128
TC = 512
NC_T = L // TC
ENGINES = ("sync", "scalar", "vector", "tensor", "gpsimd")



class Item:
    __slots__ = ("eng", "fn", "deps", "need_inc", "sem_val", "is_dma",
                 "dsem", "dval", "pre_waits", "tag", "extra_inc", "fence")

    def __init__(self, eng, fn, tag=""):
        self.eng = eng
        self.fn = fn
        self.deps = set()
        self.need_inc = False
        self.sem_val = 0
        self.is_dma = False
        self.dsem = None
        self.dval = 0
        self.pre_waits = ()
        self.tag = tag
        self.extra_inc = None
        self.fence = False


class Buf:
    __slots__ = ("writer", "readers", "name")

    def __init__(self, name=""):
        self.writer = None
        self.readers = []
        self.name = name


class _DmaSem:
    __slots__ = ("sem", "uses")

    def __init__(self, sem):
        self.sem = sem
        self.uses = 0


class Sched:
    def __init__(self, nc, es):
        self.nc = nc
        self.items = {e: [] for e in ENGINES}
        self.prog_sem = {e: es.enter_context(nc.semaphore("pg_" + e))
                         for e in ENGINES}
        self.dma_pools = {}
        for e, n in (("sync", 12), ("gpsimd", 12), ("scalar", 4)):
            self.dma_pools[e] = [_DmaSem(es.enter_context(nc.semaphore(f"dm_{e}{i}")))
                                 for i in range(n)]
        self.dma_rr = {e: 0 for e in self.dma_pools}
        self.last_barrier = []
        self._es = es
        self._barn = 0
        self.vfence = False

    def _dep(self, item, dep):
        if dep is None or dep is item:
            return
        if dep.eng == item.eng and not dep.is_dma:
            if SIM_EDGES or dep.fence or dep.eng in EDGE_ENGS:
                item.deps.add(dep)
                dep.need_inc = True
            return
        item.deps.add(dep)
        if not dep.is_dma:
            dep.need_inc = True

    def add(self, eng, fn, reads=(), writes=(), dma=False, tag="", deps=()):
        it = Item(eng, fn, tag)
        if dma:
            it.is_dma = True
            pool = self.dma_pools[eng]
            ds = pool[self.dma_rr[eng] % len(pool)]
            self.dma_rr[eng] += 1
            if ds.uses:
                it.pre_waits = ((ds.sem, 16 * ds.uses),)
            ds.uses += 1
            it.dsem, it.dval = ds, 16 * ds.uses
        for d in deps:
            self._dep(it, d)
        for b in reads:
            self._dep(it, b.writer)
            b.readers.append(it)
        for b in writes:
            self._dep(it, b.writer)
            for r in b.readers:
                self._dep(it, r)
            b.readers = []
            b.writer = it
        if eng == "vector" and (self.vfence
                                or len(self.items[eng]) < FENCE_N):
            it.fence = True
        self.items[eng].append(it)
        return it

    def barrier(self, engines=("scalar", "vector", "tensor"), extra_deps=()):
        bsem = self._es.enter_context(self.nc.semaphore(f"bar{self._barn}"))
        self._barn += 1
        n = len(engines)
        for e in engines:
            it = self.add(e, lambda eng: eng.drain(), tag="bar")
            for d in extra_deps:
                self._dep(it, d)
            it.extra_inc = bsem
        for e in engines:
            w = Item(e, None, "barw")
            w.pre_waits = ((bsem, n),)
            self.items[e].append(w)
        self.last_barrier = bsem
        return bsem

    def finalize_and_emit(self):
        for e in ENGINES:
            c = 0
            for it in self.items[e]:
                if it.need_inc and not it.is_dma:
                    c += 1
                    it.sem_val = c
        sched = self
        with self.nc.Block() as block:
            def mk(e, items):
                def body(eng):
                    hwm = {}
                    for it in items:
                        ws = list(it.pre_waits)
                        for d in it.deps:
                            if d.is_dma:
                                ws.append((d.dsem.sem, d.dval))
                            else:
                                ws.append((sched.prog_sem[d.eng], d.sem_val))
                        for sem, val in ws:
                            k = id(sem)
                            if hwm.get(k, -1) >= val:
                                continue
                            hwm[k] = val
                            eng.wait_ge(sem, val)
                        inst = it.fn(eng) if it.fn is not None else None
                        if inst is None:
                            continue
                        if it.is_dma:
                            inst.then_inc(it.dsem.sem, 16)
                        elif it.extra_inc is not None:
                            inst.then_inc(it.extra_inc, 1)
                        elif it.need_inc:
                            inst.then_inc(sched.prog_sem[e], 1)
                return body
            for e in ENGINES:
                getattr(block, e)(mk(e, self.items[e]))


FFN_SUB = 99
import os as _os
SIM_EDGES = _os.environ.get("BASS_SIM_FP64") == "1"
EDGE_ENGS = set(
    e for e in _os.environ.get("MK_EDGE_ENGS", "").split(",") if e)
FENCE_N = int(_os.environ.get("MK_FENCE_N", "0"))


def build_program(dbg=(), stages=3):
    nc = bass.Bass("TRN2", target_bir_lowering=False, debug=False)
    es = ExitStack()

    def din(name, shape):
        return nc.dram_tensor(name, list(shape), F32, kind="ExternalInput").ap()

    x_h = din("x", (128, NK * L))
    inw_h = {d: din(f"inw_{d}", (32, 128, 1024)) for d in "fb"}
    outw_h = {d: din(f"outw_{d}", (NM, 128, NJ * 128)) for d in "fb"}
    mrgw_h = din("mrgw", (NM, 128, 16 * 128))
    xprj_h = {d: din(f"xprj_{d}", (128, NJ * 96)) for d in "fb"}
    dtw_h = {d: din(f"dtw_{d}", (64, DI)) for d in "fb"}
    alog_h = din("alog", (128, 2 * NJ * DS))
    dpar_h = din("dpar", (128, 2 * NJ))
    dtb_h = din("dtb", (128, 2 * NJ))
    cw_h = din("cw", (128, 2 * NJ * 4))
    cbp_h = din("cbp", (128, 2 * NJ))
    lng_h = din("lng", (128, NK))
    lnb_h = din("lnb", (128, NK))
    mrgb_h = din("mrgb", (128, NM))
    fw_h = din("fw", (NK, 128, 1024))
    fb_h = din("fb", (128, NK))
    fbr_h = din("fbr", (1, 1024))

    out_h = nc.dram_tensor("out", [128, NK * L], F32, kind="ExternalOutput").ap()
    xsp_h = nc.dram_tensor("xspill", [128, NK * L], F32, kind="Internal").ap()
    mrow_h = nc.dram_tensor("mrow", [1, 6144], F32, kind="Internal").ap()
    dbg_h = {name: nc.dram_tensor("dbg_" + name, list(shape), F32,
                                  kind="ExternalOutput").ap()
             for name, shape in dbg}

    def sb(name, shape, dt):
        return es.enter_context(nc.sbuf_tensor(name, list(shape), dt))

    X = sb("X", (128, NK * L), F32)
    XV = X.ap().bitcast(F16)
    XI0 = 16384
    OB0 = 24576
    XT = sb("XTxn2", (128, 16 * 1024), F16)
    XD = sb("XDoutf", (128, NK * L), F16)
    SC16 = sb("sc16", (128, 6784), F16)
    SC32 = sb("sc32", (128, 576), F32)
    WDF = sb("wdf", (128, NK * 1024), F16)
    NEGA = sb("negA", (128, 2 * NJ * DS), F32)
    SPAR = sb("spar", (128, 264), F32)
    EYE = sb("eye", (128, 128), F16)
    EYEN = sb("eyen", (128, 128), F16)
    IOP = sb("iop", (128, 1), F32)
    IOF = sb("iof", (128, 128), F32)
    W1024 = sb("w1024", (128, 1), F16)
    CAPS = sb("caps", (128, 2), F32)
    SROW32 = sb("srow32", (1, 1536), F32)
    STAT = sb("stat", (128, 256), F32)
    INWS = sb("inws", (128, 3 * 1024), F16)
    OUTWS = sb("outws", (128, 2 * 2048), F16)
    MRGS = sb("mrgs", (128, 2048), F16)
    ONER = sb("oner", (1, 512), F16)
    FBR = sb("fbrs", (1, 1024), F16)
    XPS = sb("xps", (128, NJ * 96), F16)
    DTS = sb("dts", (64, DI), F16)
    DGW = sb("dgw", (128, 8 * 128), F16)
    XST = sb("xst", (128, 2 * TC), F32)
    DGD = SC16[:, 2568:2696]
    A23 = sb("a23", (128, 512), F16)
    H2 = sb("h2", (128, 1024), F16)
    MUB = None

    PS = [es.enter_context(nc.psum_tensor(f"pb{i}", [128, 512], F32))
          for i in range(6)]
    PT16 = es.enter_context(nc.psum_tensor("pt16", [128, 1024], F16))
    PS7 = es.enter_context(nc.psum_tensor("pb7", [128, 512], F32))

    S = Sched(nc, es)
    bufs = {}

    def B(name):
        if name not in bufs:
            bufs[name] = Buf(name)
        return bufs[name]

    pbuf = [B(f"ps{i}") for i in range(6)]
    pt16b = B("pt16")
    ps7b = B("ps7")

    def dma(eng, out_ap, in_ap, reads=(), writes=(), tag="", deps=()):
        return S.add(eng, lambda E, o=out_ap, i=in_ap: E.dma_start(out=o, in_=i),
                     reads=reads, writes=writes, dma=True, tag=tag, deps=deps)

    def vop(fn, reads=(), writes=(), tag=""):
        return S.add("vector", fn, reads=reads, writes=writes, tag=tag)

    def sop(fn, reads=(), writes=(), tag=""):
        return S.add("scalar", fn, reads=reads, writes=writes, tag=tag)

    def top(fn, reads=(), writes=(), tag=""):
        return S.add("tensor", fn, reads=reads, writes=writes, tag=tag)

    dma("sync", X[:, :], x_h[:, :], writes=[B("X")], tag="ldx")
    dma("gpsimd", NEGA[:, :], alog_h[:, :], writes=[B("nega")])
    sop(lambda E: E.activation(out=NEGA[:, :], in_=NEGA[:, :], func=ACT.Exp),
        reads=[B("nega")], writes=[B("nega")])
    dma("gpsimd", SPAR[:, 0:32], dpar_h[:, :], writes=[B("spD")])
    dma("gpsimd", SPAR[:, 32:64], dtb_h[:, :], writes=[B("sp_ndtb")])
    sop(lambda E: E.mul(out=SPAR[:, 32:64], in_=SPAR[:, 32:64], mul=-1.0),
        reads=[B("sp_ndtb")], writes=[B("sp_ndtb")])
    dma("gpsimd", SPAR[:, 64:192], cw_h[:, :], writes=[B("spcw")])
    dma("gpsimd", SPAR[:, 192:200], lng_h[:, :], writes=[B("splng")])
    dma("gpsimd", SPAR[:, 200:208], lnb_h[:, :], writes=[B("splnb")])
    dma("gpsimd", SPAR[:, 208:216], mrgb_h[:, :], writes=[B("spmb")])
    dma("gpsimd", SPAR[:, 224:232], fb_h[:, :], writes=[B("spfb")])
    dma("gpsimd", FBR[:, :], fbr_h[:, :], writes=[B("fbr")])
    dma("gpsimd", SPAR[:, 232:264], cbp_h[:, :], writes=[B("spcb")])
    S.add("gpsimd", lambda E: E.iota(IOP[:, :], [[0, 1]], channel_multiplier=1,
                                     allow_small_or_imprecise_dtypes=True),
          writes=[B("iop")])
    S.add("gpsimd", lambda E: E.iota(IOF[:, :], [[1, 128]], channel_multiplier=0,
                                     allow_small_or_imprecise_dtypes=True),
          writes=[B("iof")])
    vop(lambda E: E.tensor_tensor(EYE[:, :], IOP[:, :].broadcast_to([128, 128]),
                                  IOF[:, :], ALU.is_equal),
        reads=[B("iop"), B("iof")], writes=[B("eye")])
    vop(lambda E: E.tensor_scalar(EYEN[:, :], EYE[:, :], -1.0, None, ALU.mult),
        reads=[B("eye")], writes=[B("eyen")])
    vop(lambda E: E.memset(W1024[:, :], 1.0 / 1024.0), writes=[B("w1024")])
    vop(lambda E: E.memset(ONER[:, :], 1.0), writes=[B("oner")])
    vop(lambda E: E.memset(CAPS[:, 0:1], 1e-5), writes=[B("caps")])
    vop(lambda E: E.memset(CAPS[:, 1:2], 1e-30), writes=[B("caps")])

    BF16 = mybir.dt.bfloat16
    I32 = mybir.dt.int32

    def grid_round(y32, y32b, W, out_ap, out_buf, fence_all=False):
        def gvop(*a, **k):
            it = vop(*a, **k)
            if fence_all:
                it.fence = True
            return it
        qa = SC16[:, 0:W]
        q = SC16[:, 1024:1024 + W]
        qb2 = SC16[:, 2048:2048 + W]
        m = SC16[:, 3072:3072 + W]
        ys = SC16[:, 4096:4096 + W]
        qb8 = SC16[:, 5120:5120 + W // 2].bitcast(F8)[:, 0:W]
        qab, qb_, qb2b = B("g_qa"), B("g_q"), B("g_qb2")
        mb, ysb = B("g_m"), B("g_ys")
        gvop(lambda E: E.tensor_copy(ys, y32), reads=[y32b], writes=[ysb])
        gvop(lambda E: E.tensor_scalar(y32.bitcast(I32), y32.bitcast(I32),
                                      0x7FFFFFFF, None, ALU.bitwise_and),
            reads=[y32b], writes=[y32b])
        gvop(lambda E: E.tensor_scalar(m, y32, 1.5, None, ALU.is_lt),
            reads=[y32b], writes=[mb])
        gvop(lambda E: E.tensor_scalar(qa.bitcast(BF16), y32, 16.0, None, ALU.add),
            reads=[y32b], writes=[qab])
        gvop(lambda E: E.tensor_scalar(qa, qa.bitcast(BF16), -16.0, None, ALU.add),
            reads=[qab], writes=[qab])
        gvop(lambda E: E.tensor_copy(qb8, y32), reads=[y32b], writes=[qb2b])
        gvop(lambda E: E.tensor_copy(qb2, qb8), reads=[qb2b], writes=[qb2b])
        gvop(lambda E: E.tensor_tensor(qa, qa, qb2, ALU.subtract),
            reads=[qab, qb2b], writes=[qab])
        gvop(lambda E: E.tensor_tensor(m, m, qa, ALU.mult),
            reads=[mb, qab], writes=[mb])
        gvop(lambda E: E.tensor_tensor(q, qb2, m, ALU.add),
            reads=[qb2b, mb], writes=[qb_])
        gvop(lambda E: E.tensor_scalar(ys.bitcast(I16), ys.bitcast(I16),
                                      -0x8000, None, ALU.bitwise_and),
            reads=[ysb], writes=[ysb])
        gvop(lambda E: E.tensor_tensor(out_ap.bitcast(I16), q.bitcast(I16),
                                      ys.bitcast(I16), ALU.bitwise_or),
            reads=[qb_, ysb], writes=[out_buf])

    XTF = XT.ap().bitcast(F32)
    XDSC = SC16[:, 5632:6656]

    S.vfence = True
    for ot in range(NK):
        wslot = XTF[:, (ot % 2) * 1024:(ot % 2) * 1024 + 1024]
        wb = B(f"w32{ot % 2}")
        dma("sync", wslot, fw_h[ot, :, :], writes=[wb], tag="fw")
        bm = SC32[:, 512:544]
        rb = SC32[:, 544:576]
        bmb, rbb = B("q_bm"), B("q_rb")
        vop(lambda E, wslot=wslot: E.tensor_reduce(
            bm, wslot.rearrange("p (g k) -> p g k", k=32),
            mybir.AxisListType.X, ALU.max, apply_absolute_value=True),
            reads=[wb], writes=[bmb])
        vop(lambda E: E.tensor_scalar(bm, bm, 7.5e-30, None, ALU.max),
            reads=[bmb], writes=[bmb])
        vop(lambda E: E.reciprocal(rb, bm), reads=[bmb], writes=[rbb])
        vop(lambda E: E.tensor_scalar(rb, rb, 7.5, None, ALU.mult),
            reads=[rbb], writes=[rbb])
        yw = XTF[:, 2048:3072]
        ywb = B("w_y")
        vop(lambda E, wslot=wslot: E.tensor_tensor(
            yw.rearrange("p (g k) -> p g k", k=32),
            wslot.rearrange("p (g k) -> p g k", k=32),
            rb.rearrange("p (g o) -> p g o", o=1).broadcast_to([128, 32, 32]),
            ALU.mult),
            reads=[wb, rbb], writes=[ywb])
        grid_round(yw, ywb, 1024, XDSC[:, 0:1024], B("g_out"))
        vop(lambda E: E.tensor_scalar(bm, bm, 1.0 / 7.5, None, ALU.mult),
            reads=[bmb], writes=[bmb])
        vop(lambda E, ot=ot: E.tensor_tensor(
            XD[:, ot * 1024:(ot + 1) * 1024].rearrange("p (g k) -> p g k", k=32),
            XDSC[:, 0:1024].rearrange("p (g k) -> p g k", k=32),
            bm.rearrange("p (g o) -> p g o", o=1).broadcast_to([128, 32, 32]),
            ALU.mult),
            reads=[B("g_out"), bmb], writes=[B(f"fwd{ot}")])
    for kt in range(NK):
        for ot in range(NK):
            _it = top(lambda E, kt=kt, ot=ot: E.transpose(
                PT16[:, ot * 128:(ot + 1) * 128],
                XD[:, ot * 1024 + kt * 128: ot * 1024 + (kt + 1) * 128], EYE[:, :]),
                reads=[B(f"fwd{ot}"), B("eye")], writes=[pt16b])
            _it.fence = True
        sop(lambda E, kt=kt: E.copy(out=WDF[:, kt * 1024:(kt + 1) * 1024],
                                    in_=PT16[:, :]),
            reads=[pt16b], writes=[B(f"wdf{kt}")])
    S.vfence = False

    MUB = XST.ap()

    def ffn_block(callid):
        S.vfence = True
        u = f"f{callid}"
        for kt in range(NK):
            guards = [B(f"fwd{kt * 2}"), B(f"fwd{kt * 2 + 1}")] if callid == 0 else []
            vop(lambda E, kt=kt: E.tensor_copy(
                XD[:, kt * L:(kt + 1) * L], X[:, kt * L:(kt + 1) * L]),
                reads=[B("X")], writes=[B(f"xb{kt}" + u)] + guards)
        def ffn_mms(c):
            for m in range(NM):
                bk = 2 + (c * NM + m) % 4
                PB = PS[bk]
                pb = pbuf[bk]
                top(lambda E, m=m, PB=PB: E.matmul(
                    PB[:, 0:512], FBR[:, m * 128:(m + 1) * 128], ONER[:, :],
                    start=True, stop=False),
                    reads=[B("fbr"), B("oner")], writes=[pb])
                for kt in range(NK):
                    top(lambda E, m=m, c=c, kt=kt, PB=PB: E.matmul(
                        PB[:, 0:512],
                        WDF[:, kt * 1024 + m * 128: kt * 1024 + (m + 1) * 128],
                        XD[:, kt * L + c * TC: kt * L + (c + 1) * TC],
                        start=False, stop=(kt == NK - 1)),
                        reads=[B(f"wdf{kt}"), B(f"xdf{kt}{c}" + u)],
                        writes=[pb])
                vop(lambda E, m=m, c=c, PB=PB: E.scalar_tensor_tensor(
                    X[:, m * L + c * TC: m * L + (c + 1) * TC],
                    PB[:, 0:512], 0.5,
                    X[:, m * L + c * TC: m * L + (c + 1) * TC],
                    ALU.mult, ALU.add),
                    reads=[pb, B("X")], writes=[B("X")])
        for c in range(NC_T):
            x2g = [B("w320"), B("w321"), B("w_y")] if (callid == 0 and c < 2) else []
            for kt in range(NK):
                sop(lambda E, kt=kt, c=c: E.activation(
                    out=XT[:, c * 4096 + kt * 512: c * 4096 + (kt + 1) * 512],
                    in_=X[:, kt * L + c * TC: kt * L + (c + 1) * TC],
                    func=ACT.Square),
                    reads=[B("X")], writes=[B(f"x2_{c}" + u)] + x2g)
            for kt in range(NK):
                top(lambda E, kt=kt, c=c: E.matmul(
                    PS7[0:1, 0:512], W1024[:, :],
                    XD[:, kt * L + c * TC: kt * L + (c + 1) * TC],
                    start=(kt == 0), stop=(kt == NK - 1)),
                    reads=[B(f"xb{kt}" + u), B("w1024")], writes=[ps7b])
            sop(lambda E: E.copy(out=SROW32[:, 0:512], in_=PS7[0:1, 0:512]),
                reads=[ps7b], writes=[B("murow")])
            for kt in range(NK):
                top(lambda E, kt=kt, c=c: E.matmul(
                    PS7[0:1, 0:512], W1024[:, :],
                    XT[:, c * 4096 + kt * 512: c * 4096 + (kt + 1) * 512],
                    start=(kt == 0), stop=(kt == NK - 1)),
                    reads=[B(f"x2_{c}" + u), B("w1024")], writes=[ps7b])
            sop(lambda E: E.activation(out=SROW32[:, 512:1024],
                                       in_=SROW32[:, 0:512], func=ACT.Square),
                reads=[B("murow")], writes=[B("varrow")])
            vop(lambda E: E.tensor_tensor(SROW32[:, 512:1024], PS7[0:1, 0:512],
                                          SROW32[:, 512:1024], ALU.subtract),
                reads=[ps7b, B("varrow")], writes=[B("varrow")])
            sop(lambda E: E.activation(out=SROW32[:, 512:1024],
                                       in_=SROW32[:, 512:1024], func=ACT.Sqrt,
                                       bias=CAPS[0:1, 0:1]),
                reads=[B("varrow"), B("caps")], writes=[B("varrow")])
            vop(lambda E: E.reciprocal(SROW32[:, 512:1024], SROW32[:, 512:1024]),
                reads=[B("varrow")], writes=[B("varrow")])
            dma("sync", mrow_h[:, 0:1024], SROW32[:, 0:1024],
                reads=[B("murow"), B("varrow")], writes=[B("mrowh")], tag="rowst")
            dma("sync", MUB[:, 0:512], mrow_h[0:1, 0:512].partition_broadcast(128),
                reads=[B("mrowh")], writes=[B("mub")], tag="rowbc")
            dma("sync", MUB[:, 512:1024],
                mrow_h[0:1, 512:1024].partition_broadcast(128),
                reads=[B("mrowh")], writes=[B("rsb")], tag="rowbc")
            fo = c * 2048
            A = XTF[:, fo:fo + 512]
            MS = XTF[:, fo + 512:fo + 1024]
            RS = XTF[:, fo + 1024:fo + 1536]
            D = XTF[:, fo + 1536:fo + 2048]
            Ab, MSb, RSb, Db = B("scrA"), B("scrMS"), B("scrRS"), B("scrD")
            _it = vop(lambda E, MS=MS: E.transpose(MS, MUB[:, 0:512]),
                reads=[B("mub")], writes=[MSb, B(f"x2_{c}" + u)])
            _it.fence = True
            _it = vop(lambda E, RS=RS: E.transpose(RS, MUB[:, 512:1024]),
                reads=[B("rsb")], writes=[RSb])
            _it.fence = True
            bm = SC32[:, 512:528]
            rb = SC32[:, 528:544]
            sc = SC32[:, 544:560]
            bmb, rbb, scb = B("x_bm"), B("x_rb"), B("x_sc")
            for kt in range(NK):
                _it = vop(lambda E, kt=kt, c=c, A=A: E.transpose(
                    A, X[:, kt * L + c * TC: kt * L + (c + 1) * TC]),
                    reads=[B("X")], writes=[Ab])
                _it.fence = True
                vop(lambda E, A=A, MS=MS: E.tensor_tensor(A, A, MS, ALU.subtract),
                    reads=[Ab, MSb], writes=[Ab])
                _it = vop(lambda E, A=A: E.tensor_reduce(
                    bm, A.rearrange("p (g k) -> p g k", k=32),
                    mybir.AxisListType.X, ALU.max, apply_absolute_value=True),
                    reads=[Ab], writes=[bmb])
                _it.fence = True
                vop(lambda E: E.tensor_scalar(bm, bm, 7.5e-30, None, ALU.max),
                    reads=[bmb], writes=[bmb])
                _it = vop(lambda E: E.reciprocal(rb, bm), reads=[bmb], writes=[rbb])
                _it.fence = True
                vop(lambda E: E.tensor_scalar(rb, rb, 7.5, None, ALU.mult),
                    reads=[rbb], writes=[rbb])
                vop(lambda E, A=A, D=D: E.tensor_tensor(
                    D.rearrange("p (g k) -> p g k", k=32),
                    A.rearrange("p (g k) -> p g k", k=32),
                    rb.rearrange("p (g o) -> p g o", o=1).broadcast_to([128, 16, 32]),
                    ALU.mult),
                    reads=[Ab, rbb], writes=[Db])
                grid_round(D, Db, 512, XDSC[:, 0:512], B("g_out"),
                           fence_all=_os.environ.get("MK_GRF", "") == "1")
                vop(lambda E, RS=RS: E.tensor_tensor(
                    sc, bm, RS[:, 0:512:32], ALU.mult),
                    reads=[bmb, RSb], writes=[scb])
                vop(lambda E: E.tensor_scalar(sc, sc, 1.0 / 7.5, None, ALU.mult),
                    reads=[scb], writes=[scb])
                vop(lambda E: E.tensor_tensor(
                    XDSC[:, 0:512].rearrange("p (g k) -> p g k", k=32),
                    XDSC[:, 0:512].rearrange("p (g k) -> p g k", k=32),
                    sc.rearrange("p (g o) -> p g o", o=1).broadcast_to([128, 16, 32]),
                    ALU.mult),
                    reads=[B("g_out"), scb], writes=[B("g_out")])
                _it = vop(lambda E, kt=kt, c=c: E.transpose(
                    XD[:, kt * L + c * TC: kt * L + (c + 1) * TC], XDSC[:, 0:512]),
                    reads=[B("g_out")], writes=[B(f"xdf{kt}{c}" + u), B(f"xb{kt}" + u)])
                _it.fence = True
            ffn_mms(c)
        S.vfence = False


    def mixer():
        for kt in range(NK):
            vop(lambda E, kt=kt: E.tensor_copy(
                XT[:, kt * L:(kt + 1) * L], X[:, kt * L:(kt + 1) * L]),
                reads=[B("X")], writes=[B(f"mxb{kt}")])
            sop(lambda E, kt=kt: E.activation(
                out=XD[:, kt * L:(kt + 1) * L], in_=X[:, kt * L:(kt + 1) * L],
                func=ACT.Square),
                reads=[B("X")], writes=[B("mx2")])
        for c in range(NC_T):
            for kt in range(NK):
                top(lambda E, c=c, kt=kt: E.matmul(
                    PS7[0:1, 0:512], W1024[:, :],
                    XT[:, kt * L + c * TC: kt * L + c * TC + TC],
                    start=(kt == 0), stop=(kt == NK - 1)),
                    reads=[B(f"mxb{kt}"), B("w1024")], writes=[ps7b])
            sop(lambda E: E.copy(out=SROW32[:, 0:512], in_=PS7[0:1, 0:512]),
                reads=[ps7b], writes=[B("murow")])
            for kt in range(NK):
                top(lambda E, c=c, kt=kt: E.matmul(
                    PS7[0:1, 0:512], W1024[:, :],
                    XD[:, kt * L + c * TC: kt * L + c * TC + TC],
                    start=(kt == 0), stop=(kt == NK - 1)),
                    reads=[B("mx2"), B("w1024")], writes=[ps7b])
            VROW = SC32[0:1, 0:512]
            sop(lambda E, VROW=VROW: E.activation(out=VROW,
                                       in_=SROW32[:, 0:512], func=ACT.Square),
                reads=[B("murow")], writes=[B("varrow"), B("sig")])
            vop(lambda E, VROW=VROW: E.tensor_tensor(VROW, PS7[0:1, 0:512],
                                          VROW, ALU.subtract),
                reads=[ps7b, B("varrow")], writes=[B("varrow"), B("sig")])
            sop(lambda E, VROW=VROW: E.activation(out=SROW32[:, 512:1024],
                                       in_=VROW, func=ACT.Sqrt,
                                       bias=CAPS[0:1, 0:1]),
                reads=[B("varrow"), B("caps")], writes=[B("rstd1")])
            vop(lambda E: E.reciprocal(SROW32[:, 512:1024], SROW32[:, 512:1024]),
                reads=[B("rstd1")], writes=[B("rstd1")])
            sop(lambda E: E.activation(out=SROW32[:, 1024:1536],
                                       in_=SROW32[:, 512:1024], func=ACT.Square),
                reads=[B("rstd1")], writes=[B("reff")])
            vop(lambda E, VROW=VROW: E.tensor_tensor(SROW32[:, 1024:1536],
                                          VROW,
                                          SROW32[:, 1024:1536], ALU.mult),
                reads=[B("varrow"), B("reff")], writes=[B("reff")])
            sop(lambda E: E.activation(out=SROW32[:, 1024:1536],
                                       in_=SROW32[:, 1024:1536], func=ACT.Sqrt,
                                       bias=CAPS[0:1, 0:1]),
                reads=[B("reff"), B("caps")], writes=[B("reff")])
            vop(lambda E: E.reciprocal(SROW32[:, 1024:1536], SROW32[:, 1024:1536]),
                reads=[B("reff")], writes=[B("reff")])
            vop(lambda E: E.tensor_tensor(SROW32[:, 1024:1536], SROW32[:, 512:1024],
                                          SROW32[:, 1024:1536], ALU.mult),
                reads=[B("rstd1"), B("reff")], writes=[B("reff")])
            dma("sync", mrow_h[:, c * 1536:(c + 1) * 1536], SROW32[:, 0:1536],
                reads=[B("murow"), B("rstd1"), B("reff")], writes=[B("mrowh")],
                tag="rowst")
            dma("sync", MUB[:, 0:512],
                mrow_h[0:1, c * 1536:c * 1536 + 512].partition_broadcast(128),
                reads=[B("mrowh")], writes=[B("mub")], tag="rowbc")
            dma("sync", MUB[:, 512:1024],
                mrow_h[0:1, c * 1536 + 1024:c * 1536 + 1536]
                .partition_broadcast(128),
                reads=[B("mrowh")], writes=[B("rsb")], tag="rowbc")
            for kt in range(NK):
                vop(lambda E, kt=kt, c=c: E.tensor_tensor(
                    SC16[:, 0:TC],
                    X[:, kt * L + c * TC: kt * L + (c + 1) * TC],
                    MUB[:, 0:512], ALU.subtract),
                    reads=[B("X"), B("mub")], writes=[B("xn2t")])
                vop(lambda E: E.tensor_tensor(
                    SC16[:, 0:TC], SC16[:, 0:TC], MUB[:, 512:1024], ALU.mult),
                    reads=[B("xn2t"), B("rsb")], writes=[B("xn2t")])
                vop(lambda E, kt=kt, c=c: E.tensor_scalar(
                    XT[:, kt * L + c * TC: kt * L + (c + 1) * TC],
                    SC16[:, 0:TC], SPAR[:, 192 + kt:193 + kt],
                    SPAR[:, 200 + kt:201 + kt], ALU.mult, ALU.add),
                    reads=[B("xn2t"), B("splng"), B("splnb")],
                    writes=[B(f"xn2_{kt}"), B(f"mxb{kt}")])
        spills = []
        for kt in range(NK):
            spills.append(dma("sync", xsp_h[:, kt * L:(kt + 1) * L],
                              X[:, kt * L:(kt + 1) * L], reads=[B("X")],
                              writes=[B(f"xsp{kt}")], tag="xspill"))
        S.barrier(extra_deps=spills)
        for di, dname in enumerate("fb"):
            mixer_dir(di, dname)
        bar = S.barrier()
        return bar

    def mixer_dir(di, dname):
        rev = (dname == "b")
        dma("gpsimd", XPS[:, :], xprj_h[dname][:, :], writes=[B("xps")])
        dma("gpsimd", DTS[:, :], dtw_h[dname][:, :], writes=[B("dts")])
        vop(lambda E: E.memset(STAT[:, :], 0.0),
            writes=[B(f"stat{q}") for q in range(DS)])
        chunks = list(range(NC_T))
        if rev:
            chunks = chunks[::-1]
        for ci, c in enumerate(chunks):
            mixer_chunk(di, dname, c, rev, first_c=(ci == 0))

    def mixer_chunk(di, dname, c, rev, first_c):
        u = f"{dname}{c}"
        t0 = c * TC
        PZ = PT16.ap().bitcast(F32)
        proj = SC16[:, 6144:6656]
        sz = SC16[:, 1544:2056]
        for j in range(NJ):
            p = j % 2
            ext = SC16[:, 512 + p * 516:1028 + p * 516]
            extb = B(f"ext{p}")
            tb = B(f"tail{j}")
            sl = INWS[:, p * 1024:p * 1024 + 1024]
            slb = B(f"inw{p}")
            ip_ap = PS[0][:, 0:512] if p == 0 else PZ[:, 0:512]
            ipb = pbuf[0] if p == 0 else pt16b
            cv_ap = PS[1][:, 0:512] if p == 0 else PS[5][:, 0:512]
            cvb = pbuf[1] if p == 0 else pbuf[5]
            dma("gpsimd", sl, inw_h[dname][j, :, :], writes=[slb], tag="inw")
            for kt in range(NK):
                top(lambda E, sl=sl, kt=kt, t0=t0, ip_ap=ip_ap: E.matmul(
                    ip_ap, sl[:, kt * 128:(kt + 1) * 128],
                    XT[:, kt * L + t0: kt * L + t0 + TC],
                    start=(kt == 0), stop=(kt == NK - 1)),
                    reads=[slb, B(f"xn2_{kt}")], writes=[ipb])
            for tap in range(4):
                vop(lambda E, tap=tap, j=j, p=p: E.tensor_scalar(
                    DGW[:, (p * 4 + tap) * 128:(p * 4 + tap + 1) * 128], EYE[:, :],
                    SPAR[:, 64 + di * 64 + j * 4 + tap:65 + di * 64 + j * 4 + tap],
                    None, ALU.mult),
                    reads=[B("eye"), B("spcw")], writes=[B(f"dgw{p}{tap}")])
            if not rev:
                sop(lambda E, ext=ext, ip_ap=ip_ap: E.copy(out=ext[:, 4:516],
                                              in_=ip_ap),
                    reads=[ipb], writes=[extb])
                if first_c:
                    sop(lambda E, ext=ext: E.memzero(ext[:, 0:4]),
                        writes=[extb])
                else:
                    sop(lambda E, ext=ext, j=j: E.copy(
                        out=ext[:, 0:4], in_=SC16[:, 6656 + j * 4:6660 + j * 4]),
                        reads=[tb], writes=[extb])
                sop(lambda E, ext=ext, j=j: E.copy(
                    out=SC16[:, 6656 + j * 4:6660 + j * 4], in_=ext[:, 512:516]),
                    reads=[extb], writes=[tb])
                taps = [ext[:, tap + 1: tap + 513] for tap in range(4)]
            else:
                sop(lambda E, ext=ext, ip_ap=ip_ap: E.copy(out=ext[:, 0:512],
                                              in_=ip_ap),
                    reads=[ipb], writes=[extb])
                if first_c:
                    sop(lambda E, ext=ext: E.memzero(ext[:, 512:516]),
                        writes=[extb])
                else:
                    sop(lambda E, ext=ext, j=j: E.copy(
                        out=ext[:, 512:516],
                        in_=SC16[:, 6656 + j * 4:6660 + j * 4]),
                        reads=[tb], writes=[extb])
                sop(lambda E, ext=ext, j=j: E.copy(
                    out=SC16[:, 6656 + j * 4:6660 + j * 4], in_=ext[:, 0:4]),
                    reads=[extb], writes=[tb])
                taps = [ext[:, 3 - tap: 515 - tap] for tap in range(4)]
            for tap in range(4):
                top(lambda E, tap=tap, ap=taps[tap], p=p, cv_ap=cv_ap: E.matmul(
                    cv_ap,
                    DGW[:, (p * 4 + tap) * 128:(p * 4 + tap + 1) * 128], ap,
                    start=(tap == 0), stop=(tap == 3)),
                    reads=[B(f"dgw{p}{tap}"), extb], writes=[cvb])
            sop(lambda E, j=j, cv_ap=cv_ap: E.activation(
                out=XV[:, XI0 + j * TC: XI0 + (j + 1) * TC], in_=cv_ap,
                func=ACT.Silu,
                bias=SPAR[:, 232 + di * NJ + j:233 + di * NJ + j]),
                reads=[cvb, B("spcb")], writes=[B(f"xi{j}")])
            top(lambda E, j=j: E.matmul(
                PS[2][0:96, 0:512], XPS[:, j * 96:(j + 1) * 96],
                XV[:, XI0 + j * TC: XI0 + (j + 1) * TC],
                start=(j == 0), stop=(j == NJ - 1)),
                reads=[B("xps"), B(f"xi{j}")], writes=[pbuf[2]])
        sop(lambda E: E.copy(out=proj[0:96, 0:512], in_=PS[2][0:96, 0:512]),
            reads=[pbuf[2]], writes=[B("proj")])
        for s in range(DS):
            top(lambda E, s=s: E.matmul(
                PS[3][:, 0:512],
                EYEN[0:96, 64 + s:65 + s].broadcast_to([96, 128]),
                proj[0:96, 0:512], start=True, stop=True),
                reads=[B("eyen"), B("proj")], writes=[pbuf[3]])
            sop(lambda E, s=s: E.copy(out=XV[:, s * TC:(s + 1) * TC],
                                      in_=PS[3][:, 0:512]),
                reads=[pbuf[3]], writes=[B(f"brep{s}")])
            top(lambda E, s=s: E.matmul(
                PS[0][:, 0:512],
                EYE[0:96, 80 + s:81 + s].broadcast_to([96, 128]),
                proj[0:96, 0:512], start=True, stop=True),
                reads=[B("eye"), B("proj")], writes=[pbuf[0]])
            sop(lambda E, s=s: E.copy(out=XV[:, (16 + s) * TC:(17 + s) * TC],
                                      in_=PS[0][:, 0:512]),
                reads=[pbuf[0]], writes=[B(f"crep{s}")])
        lns_s = [SC16[:, 0:512], XV[:, 31744:32256]]
        u_s = [SC16[:, 2056:2568], XV[:, 32256:32768]]
        b_s4 = [SC16[:, 4096 + q * 512:4608 + q * 512] for q in range(4)]
        h_aps = [H2[:, 0:512], H2[:, 512:1024]]
        a_aps = [SC16[:, 3072:3584], SC16[:, 3584:4096], A23[:, 0:512]]

        def l2_pro(j):
            p = j % 2
            top(lambda E, j=j: E.matmul(
                PS[4][:, 0:512], DTS[0:64, j * 128:(j + 1) * 128],
                proj[0:64, 0:512], start=True, stop=True),
                reads=[B("dts"), B("proj")], writes=[pbuf[4]])
            sop(lambda E, j=j: E.activation(
                out=SC32[:, 0:512], in_=PS[4][:, 0:512], func=ACT.Sigmoid,
                scale=-1.0,
                bias=SPAR[:, 32 + di * NJ + j:33 + di * NJ + j]),
                reads=[pbuf[4], B("sp_ndtb")], writes=[B("sig")])
            sop(lambda E, p=p: E.activation(out=lns_s[p], in_=SC32[:, 0:512],
                                            func=ACT.Ln, bias=CAPS[:, 1:2]),
                reads=[B("sig"), B("caps")], writes=[B(f"lns{p}")])

        def l2_u(j):
            p = j % 2
            vop(lambda E, j=j, p=p: E.tensor_tensor(
                u_s[p], lns_s[p], XV[:, XI0 + j * TC: XI0 + (j + 1) * TC],
                ALU.mult),
                reads=[B(f"lns{p}"), B(f"xi{j}")], writes=[B(f"uu{p}")])

        l2_pro(0)
        l2_u(0)
        for j in range(NJ):
            p = j % 2
            if j + 1 < NJ:
                l2_pro(j + 1)

            def mk_sz():
                sop(lambda E: E.activation(out=sz, in_=PZ[:, 0:512],
                                           func=ACT.Silu),
                    reads=[pt16b], writes=[B("sz")])
            lnsj, uj = lns_s[p], u_s[p]
            lnsb, ub = B(f"lns{p}"), B(f"uu{p}")
            slz = INWS[:, 2048:3072]
            slzb = B("inwz")
            dma("gpsimd", slz, inw_h[dname][16 + j, :, :], writes=[slzb],
                tag="inwz")
            for kt in range(NK):
                top(lambda E, slz=slz, kt=kt, t0=t0: E.matmul(
                    PZ[:, 0:512], slz[:, kt * 128:(kt + 1) * 128],
                    XT[:, kt * L + t0: kt * L + t0 + TC],
                    start=(kt == 0), stop=(kt == NK - 1)),
                    reads=[slzb, B(f"xn2_{kt}")], writes=[pt16b])
            vop(lambda E, j=j: E.tensor_scalar(
                DGD[:, :], EYE[:, :], SPAR[:, di * NJ + j:di * NJ + j + 1],
                None, ALU.mult),
                reads=[B("eye"), B("spD")], writes=[B("dgd")])
            top(lambda E, j=j: E.matmul(
                PS[5][:, 0:512], DGD[:, :],
                XV[:, XI0 + j * TC: XI0 + (j + 1) * TC],
                start=True, stop=False),
                reads=[B("dgd"), B(f"xi{j}")], writes=[pbuf[5]])
            def scopy(s):
                hp = s % 2
                h_ap = h_aps[hp]
                st = STAT[:, j * 16 + s:j * 16 + s + 1]
                src = h_ap[:, 0:1] if rev else h_ap[:, TC - 1:TC]
                sop(lambda E, st=st, src=src: E.copy(out=st, in_=src),
                    reads=[B(f"h{hp}")], writes=[B(f"stat{s}")])

            def mk_aexp(jj, s):
                sp = (jj + s) % 3
                lp = jj % 2
                sop(lambda E, jj=jj, s=s, sp=sp, lp=lp: E.activation(
                    out=a_aps[sp], in_=lns_s[lp], func=ACT.Exp,
                    scale=NEGA[:, di * 256 + jj * 16 + s:
                               di * 256 + jj * 16 + s + 1]),
                    reads=[B(f"lns{lp}"), B("nega")], writes=[B(f"a{sp}")])

            for s in range(DS):
                sp = (j + s) % 3
                hp = s % 2
                s4 = s % 4
                a_ap = a_aps[sp]
                h_ap = h_aps[hp]
                b_ap = b_s4[s4]
                ab, bb, hb = B(f"a{sp}"), B(f"b{s4}"), B(f"h{hp}")
                if j == 0 or s >= 3:
                    mk_aexp(j, s)
                if s >= 2:
                    scopy(s - 2)
                if s == 5:
                    mk_sz()
                if s % 2 == 0:
                    vop(lambda E, s=s, s4=s4, uj=uj: E.tensor_tensor(
                        SC16[:, 4096 + s4 * 512:4096 + s4 * 512 + 1024]
                        .rearrange("p (o t) -> p o t", o=2),
                        uj.rearrange("p (o t) -> p o t", o=1)
                        .broadcast_to([128, 2, 512]),
                        XV[:, s * TC:(s + 2) * TC]
                        .rearrange("p (o t) -> p o t", o=2),
                        ALU.mult),
                        reads=[ub, B(f"brep{s}"), B(f"brep{s + 1}")],
                        writes=[bb, B(f"b{s4 + 1}")])
                st = STAT[:, j * 16 + s:j * 16 + s + 1]
                if rev:
                    vop(lambda E, a_ap=a_ap, b_ap=b_ap, h_ap=h_ap, st=st:
                        E.tensor_tensor_scan(h_ap[:, ::-1], a_ap[:, ::-1],
                                             b_ap[:, ::-1], st, ALU.mult,
                                             ALU.add),
                        reads=[ab, bb, B(f"stat{s}")], writes=[hb])
                else:
                    vop(lambda E, a_ap=a_ap, b_ap=b_ap, h_ap=h_ap, st=st:
                        E.tensor_tensor_scan(h_ap, a_ap, b_ap, st, ALU.mult,
                                             ALU.add),
                        reads=[ab, bb, B(f"stat{s}")], writes=[hb])
                if s >= DS - 2:
                    vop(lambda E, s=s, b_ap=b_ap, h_ap=h_ap: E.tensor_tensor(
                        b_ap, h_ap, XV[:, (16 + s) * TC:(17 + s) * TC],
                        ALU.mult),
                        reads=[hb, B(f"crep{s}")], writes=[bb])
                    top(lambda E, s=s, b_ap=b_ap: E.matmul(
                        PS[5][:, 0:512], EYE[:, :], b_ap,
                        start=False, stop=(s == DS - 1)),
                        reads=[B("eye"), bb], writes=[pbuf[5]])
                elif s % 2 == 1:
                    q0 = (s - 1) % 4
                    vop(lambda E, s=s, q0=q0: E.tensor_tensor(
                        SC16[:, 4096 + q0 * 512:4096 + q0 * 512 + 1024],
                        H2[:, 0:1024],
                        XV[:, (15 + s) * TC:(17 + s) * TC], ALU.mult),
                        reads=[B("h0"), B("h1"),
                               B(f"crep{s - 1}"), B(f"crep{s}")],
                        writes=[B(f"b{q0}"), B(f"b{q0 + 1}")])
                    for sq in (s - 1, s):
                        top(lambda E, sq=sq: E.matmul(
                            PS[5][:, 0:512], EYE[:, :], b_s4[sq % 4],
                            start=False, stop=(sq == DS - 1)),
                            reads=[B("eye"), B(f"b{sq % 4}")], writes=[pbuf[5]])
            if j + 1 < NJ:
                l2_u(j + 1)
            scopy(DS - 2)
            if j + 1 < NJ:
                for s_pre in range(3):
                    mk_aexp(j + 1, s_pre)
            scopy(DS - 1)
            vop(lambda E, j=j: E.tensor_tensor(
                XV[:, XI0 + j * TC: XI0 + (j + 1) * TC], PS[5][:, 0:512], sz,
                ALU.mult),
                reads=[pbuf[5], B("sz")], writes=[B(f"xi{j}")])
        for m in range(NM):
            slo = OUTWS[:, (m % 2) * 2048:(m % 2) * 2048 + 2048]
            slob = B(f"outw{m % 2}")
            dma("gpsimd", slo, outw_h[dname][m, :, :], writes=[slob], tag="outw")
            op_ap = PS7[:, 0:512] if m % 2 == 0 else PS[4][:, 0:512]
            opb = ps7b if m % 2 == 0 else pbuf[4]
            for j in range(NJ):
                top(lambda E, slo=slo, j=j, op_ap=op_ap: E.matmul(
                    op_ap, slo[:, j * 128:(j + 1) * 128],
                    XV[:, XI0 + j * TC: XI0 + (j + 1) * TC],
                    start=(j == 0), stop=(j == NJ - 1)),
                    reads=[slob, B(f"xi{j}")], writes=[opb])
            if di == 0:
                sop(lambda E, m=m, t0=t0, op_ap=op_ap: E.copy(
                    out=XD[:, m * L + t0: m * L + t0 + TC], in_=op_ap),
                    reads=[opb],
                    writes=[B(f"outf{m}{c}"), B("mx2")])
            else:
                sop(lambda E, m=m, op_ap=op_ap: E.copy(
                    out=XV[:, OB0 + m * TC: OB0 + (m + 1) * TC],
                    in_=op_ap),
                    reads=[opb], writes=[B(f"outb{m}")])
        if di == 1:
            XF = X.ap()
            mbm = XF[:, 14336:14848]
            rbm = XF[:, 14848:15360]
            dma("sync", mbm,
                mrow_h[0:1, c * 1536:c * 1536 + 512].partition_broadcast(128),
                reads=[B("mrowh")], writes=[B("mubm"), B("X")], tag="mrowbc")
            dma("sync", rbm,
                mrow_h[0:1, c * 1536 + 512:c * 1536 + 1024]
                .partition_broadcast(128),
                reads=[B("mrowh")], writes=[B("rsbm")], tag="mrowbc")
            for m in range(NM):
                slm = MRGS[:, 0:2048]
                slmb = B("mrg0")
                dma("gpsimd", slm, mrgw_h[m, :, :], writes=[slmb], tag="mrgw")
                xs = XST[:, (m % 2) * TC:(m % 2) * TC + TC]
                xsb = B("mub") if m % 2 == 0 else B("rsb")
                dma("sync", xs, xsp_h[:, m * L + t0: m * L + t0 + TC],
                    reads=[B(f"xsp{m}")], writes=[xsb], tag="xrld")
                mg_ap = PS[3][:, 0:512] if m % 2 == 0 else PS7[:, 0:512]
                mgb = pbuf[3] if m % 2 == 0 else ps7b
                for k in range(NK):
                    top(lambda E, slm=slm, k=k, t0=t0, mg_ap=mg_ap: E.matmul(
                        mg_ap, slm[:, k * 128:(k + 1) * 128],
                        XD[:, k * L + t0: k * L + t0 + TC],
                        start=(k == 0), stop=False),
                        reads=[slmb, B(f"outf{k}{c}")], writes=[mgb])
                for k in range(NK):
                    top(lambda E, slm=slm, k=k, mg_ap=mg_ap: E.matmul(
                        mg_ap, slm[:, (8 + k) * 128:(9 + k) * 128],
                        XV[:, OB0 + k * TC: OB0 + (k + 1) * TC],
                        start=False, stop=(k == NK - 1)),
                        reads=[slmb, B(f"outb{k}")], writes=[mgb])
                t1 = XF[:, 15360:15872]
                t1b = B("mg_t1")
                vop(lambda E, xs=xs, t1=t1: E.tensor_tensor(
                    t1, xs, mbm, ALU.subtract),
                    reads=[xsb, B("mubm")], writes=[t1b])
                vop(lambda E, t1=t1: E.tensor_tensor(t1, t1, rbm, ALU.mult),
                    reads=[t1b, B("rsbm")], writes=[t1b])
                vop(lambda E, xs=xs, m=m, mg_ap=mg_ap: E.scalar_tensor_tensor(
                    xs, mg_ap, SPAR[:, 208 + m:209 + m], xs,
                    ALU.add, ALU.add),
                    reads=[mgb, xsb, B("spmb")], writes=[xsb])
                vop(lambda E, xs=xs, t1=t1: E.tensor_tensor(xs, xs, t1, ALU.add),
                    reads=[xsb, t1b], writes=[xsb])
                dma("sync", xsp_h[:, m * L + t0: m * L + t0 + TC], xs,
                    reads=[xsb], writes=[B(f"xsp{m}")], tag="xwr")

    sm = stages if stages > 4 else {0: 0, 1: 1, 2: 3, 3: 7}.get(stages, 7)
    if sm & 1:
        ffn_block(0)
        S.barrier()
    if sm & 2:
        bar = mixer()
        for kt in range(NK):
            it = dma("sync", X[:, kt * L:(kt + 1) * L],
                     xsp_h[:, kt * L:(kt + 1) * L],
                     reads=[B(f"xsp{kt}")], writes=[B("X")], tag="xrld2")
            it.pre_waits = tuple(it.pre_waits) + ((bar, 3),)
    if sm & 4:
        ffn_block(1)
    st_items = []
    for kt in range(NK):
        st_items.append(dma("sync", out_h[:, kt * L:(kt + 1) * L],
                            X[:, kt * L:(kt + 1) * L], reads=[B("X")], tag="st"))
    fin = Item("gpsimd", None, "fin")
    for it in st_items:
        fin.deps.add(it)
    S.items["gpsimd"].append(fin)

    S.finalize_and_emit()
    es.close()
    return nc



def _tile_fm(a):
    D, T = a.shape
    return np.ascontiguousarray(
        a.reshape(D // 128, 128, T).transpose(1, 0, 2).reshape(128, -1))


def _untile_fm(a, D, T):
    return np.ascontiguousarray(
        a.reshape(128, D // 128, T).transpose(1, 0, 2).reshape(D, T))


def _tile_lhsT(wT):
    K, M = wT.shape
    nk, nm = K // 128, M // 128
    a = wT.reshape(nk, 128, nm, 128).transpose(2, 1, 0, 3)
    return np.ascontiguousarray(a.reshape(nm, 128, nk * 128))


def _col_pack(v):
    return np.ascontiguousarray(v.reshape(-1, 128).T)


_NC_CACHE = {}


def _get_nc():
    import os
    global FFN_SUB
    FFN_SUB = int(os.environ.get("MK_FFN_SUB", "99"))
    st = int(os.environ.get("MK_STAGES", "3"))
    key = (st, FFN_SUB)
    if key not in _NC_CACHE:
        _NC_CACHE[key] = build_program(stages=st)
    return _NC_CACHE[key]


def kernel(**inp):
    f = {k: np.ascontiguousarray(np.asarray(v, dtype=np.float32))
         for k, v in inp.items()}
    x = f["L0_attn_x"]
    nb = x.shape[0]

    shared = {}
    for d, pre in (("f", "L0_attn_fwd_"), ("b", "L0_attn_bwd_")):
        shared[f"inw_{d}"] = _tile_lhsT(np.ascontiguousarray(f[pre + "in_w"].T))
        shared[f"outw_{d}"] = _tile_lhsT(np.ascontiguousarray(f[pre + "out_w"].T))
        wT = np.ascontiguousarray(f[pre + "xproj_w"].T)
        shared[f"xprj_{d}"] = np.ascontiguousarray(
            wT.reshape(NJ, 128, 96).transpose(1, 0, 2).reshape(128, NJ * 96))
        shared[f"dtw_{d}"] = np.ascontiguousarray(f[pre + "dt_w"].T)
    shared["mrgw"] = _tile_lhsT(np.ascontiguousarray(f["L0_attn_merge_w"].T))

    def pack2(vf, vb):
        return np.ascontiguousarray(
            np.concatenate([_col_pack(vf), _col_pack(vb)], axis=1))

    shared["alog"] = np.ascontiguousarray(np.concatenate(
        [f["L0_attn_fwd_A_log"].reshape(NJ, 128, DS).transpose(1, 0, 2)
         .reshape(128, NJ * DS),
         f["L0_attn_bwd_A_log"].reshape(NJ, 128, DS).transpose(1, 0, 2)
         .reshape(128, NJ * DS)], axis=1))
    shared["dpar"] = pack2(f["L0_attn_fwd_D"], f["L0_attn_bwd_D"])
    shared["dtb"] = pack2(f["L0_attn_fwd_dt_b"], f["L0_attn_bwd_dt_b"])
    shared["cw"] = np.ascontiguousarray(np.concatenate(
        [f["L0_attn_fwd_conv_w"].reshape(NJ, 128, 4).transpose(1, 0, 2)
         .reshape(128, NJ * 4),
         f["L0_attn_bwd_conv_w"].reshape(NJ, 128, 4).transpose(1, 0, 2)
         .reshape(128, NJ * 4)], axis=1))
    shared["cbp"] = pack2(f["L0_attn_fwd_conv_b"], f["L0_attn_bwd_conv_b"])
    shared["lng"] = _col_pack(f["L0_attn_ln_g"])
    shared["lnb"] = _col_pack(f["L0_attn_ln_b"])
    shared["mrgb"] = _col_pack(f["L0_attn_merge_b"])
    shared["fw"] = np.ascontiguousarray(f["L0_ffn_weight"].reshape(NK, 128, 1024))
    shared["fb"] = _col_pack(f["L0_ffn_bias"])
    shared["fbr"] = np.ascontiguousarray(f["L0_ffn_bias"][None, :])

    in_maps = []
    for b in range(nb):
        m = dict(shared)
        m["x"] = _tile_fm(np.ascontiguousarray(x[b].T))
        in_maps.append(m)

    nc = _get_nc()
    res = run_bass_kernel_spmd(nc, in_maps, core_ids=list(range(8)))
    outs = []
    for b in range(nb):
        o = _untile_fm(res.results[b]["out"], DM, L)
        outs.append(o.T)
    return np.stack(outs, axis=0).astype(np.float32)
